# version 18
# baseline (speedup 1.0000x reference)
import numpy as np
import ml_dtypes
from contextlib import ExitStack

import concourse.bass as bass
import concourse.mybir as mybir
from concourse.bass_utils import run_bass_kernel_spmd

F32 = mybir.dt.float32
BF16 = mybir.dt.bfloat16
I32 = mybir.dt.int32
AF = mybir.ActivationFunctionType
ALU = mybir.AluOpType

TN = 16
ND = 8
NH = 24
NCH = 4
INV127 = 1.0 / 127.0


def build_nc(phases=99):
    nc = bass.Bass("TRN2", target_bir_lowering=False)

    xin = nc.dram_tensor("xin", [128, TN * 1024], F32, kind="ExternalInput")
    cbt = nc.dram_tensor("cbt", [128, ND * 1024], F32, kind="ExternalInput")
    cuT = nc.dram_tensor("cuT", [128, ND * 1024], F32, kind="ExternalInput")
    cstf = nc.dram_tensor("cstf", [128, 514], F32, kind="ExternalInput")
    cstb = nc.dram_tensor("cstb", [128, 640], BF16, kind="ExternalInput")
    wgu = nc.dram_tensor("wgu", [NH, 128, 2048], I32, kind="ExternalInput")
    wdt = nc.dram_tensor("wdt", [128, NH * 1024], I32, kind="ExternalInput")
    out = nc.dram_tensor("out", [TN, 128, 1024], F32, kind="ExternalOutput")
    if phases < 90:
        zt_o = nc.dram_tensor("zt_o", [ND, 128, 2048], BF16, kind="ExternalOutput")
        hc_o = nc.dram_tensor("hc_o", [ND, 128, 1024], BF16, kind="ExternalOutput")
        xw_o = nc.dram_tensor("xw_o", [TN, 128, 1024], BF16, kind="ExternalOutput")
        st_o = nc.dram_tensor("st_o", [2, 128, 8], F32, kind="ExternalOutput")

    plan = {k: [] for k in ("sync", "gpsimd", "vector", "scalar", "tensor")}
    C = {"LDX": 0, "LDX2": 0, "LDX3": 0, "LDX4": 0, "LDC": 0, "WCC": 0, "WWD": 0,
         "WG0": 0, "WG1": 0, "PE": 0, "DV": 0, "SC": 0, "GP": 0, "ST": 0}
    S = {}

    def w(eng, sem, val):
        if val and val > 0:
            plan[eng].append(lambda e, s=sem, v=val: e.wait_ge(S[s], v))

    _gapn = {"vector": 0, "scalar": 0}

    def gap(eng, cyc=300):
        n = max(1, cyc // 200)
        for _ in range(n):
            i = _gapn[eng] % 3
            _gapn[eng] += 1
            if eng == "vector":
                plan[eng].append(lambda e, i=i: e.tensor_copy(GAPT[:, i:i + 1], Z128[:]))
            else:
                plan[eng].append(lambda e, i=i: e.activation(GAPS[:, i:i + 1], Z128[:],
                                                             AF.Identity, bias=0.0, scale=1.0))

    def dma(eng, out_ap, in_ap, sem):
        C[sem] += 16
        plan[eng].append(lambda e, o=out_ap, i=in_ap, s=sem:
                         e.dma_start(out=o, in_=i).then_inc(S[s], 16))
        return C[sem]

    def op(eng, fn, sem=None):
        if sem is None:
            plan[eng].append(fn)
            return None
        C[sem] += 1
        plan[eng].append(lambda e, f=fn, s=sem: f(e).then_inc(S[s], 1))
        return C[sem]

    es = ExitStack()
    for name in C:
        S[name] = es.enter_context(nc.semaphore(name))

    sb = lambda n, shp, dt: es.enter_context(nc.sbuf_tensor(n, shp, dt))

    X = [sb(f"x{t}", [128, 1024], F32) for t in range(TN)]
    XW = [sb(f"xw{t}", [128, 1024], BF16) for t in range(TN)]
    HC = [sb(f"hc{j}", [128, 1024], BF16) for j in range(ND)]
    WDTB = sb("wdtb", [128, NH * 1024], BF16)
    IDBT = sb("idb_s", [128, 128], BF16)
    M1c = sb("m1c", [128, 16], F32)
    R1c = sb("r1c", [128, 16], F32)
    NB1c = sb("nb1c", [128, 16], F32)
    R2s = sb("r2s", [128, 16], F32)
    NB2s = sb("nb2s", [128, 16], F32)
    STA = sb("sta", [128, 24], F32)
    MV1 = sb("mv1", [128, 32], F32)
    MV2 = sb("mv2", [128, 32], F32)
    WS = [sb(f"ws{i}", [128, 16], F32) for i in range(6)]
    SCR = sb("scr", [128, 12], F32)
    LMV = sb("lmv", [128, 8], F32)
    LSTA = sb("lsta", [128, 24], F32)
    EPSc = sb("epsc", [128, 1], F32)
    Z128 = sb("z128", [128, 1], F32)
    SIGc = sb("sigc", [128, 1], F32)
    LNDc = sb("lndc", [128, 1], F32)
    GAPT = sb("gapt", [128, 3], F32)
    GAPS = sb("gaps", [128, 3], F32)

    def mv_m(MV, lo, n):
        return MV[:, 0 + lo: 0 + lo + n]

    def mv_v(MV, lo, n):
        return MV[:, 16 + lo: 16 + lo + n]

    def aggr_out(MV, t):
        return MV[:, :].rearrange("p (two t) -> p two t", two=2)[:, :, t:t + 1]

    at = ExitStack()
    asb = lambda n, shp, dt: at.enter_context(nc.sbuf_tensor(n, shp, dt))
    L0B = asb("l0b", [128, 512], BF16)
    DPb = asb("dpb", [1, 512], BF16)
    D512 = asb("d512", [1, 1], F32)
    WCf = asb("wcf", [128, 4], F32)
    WCb = asb("wcb", [128, 4], BF16)
    Srowb = asb("srowb", [1, 1024], BF16)

    a2 = ExitStack()
    a2sb = lambda n, shp, dt: a2.enter_context(nc.sbuf_tensor(n, shp, dt))
    CBTB = a2sb("cbtb", [128, ND * 1024], BF16)
    CUBB = a2sb("cubb", [128, ND * 1024], BF16)
    L0F = a2sb("l0f", [128, 512], BF16)
    CSTF = a2sb("cstf_s", [128, 514], F32)
    CSTB = a2sb("cstb_s", [128, 512], BF16)

    class _CV:
        def __init__(self, T, f0, fN):
            self.T, self.f0, self.fN = T, f0, fN

        def __getitem__(self, key):
            if key == slice(None):
                return self.T[:, self.f0:self.f0 + self.fN]
            p, f = key
            fs = self.f0 + (f.start or 0)
            fe = self.f0 + (f.stop if f.stop is not None else self.fN)
            return self.T[p, fs:fe]

    E0M = _CV(CSTF, 0, 512)
    DECB = _CV(CSTF, 512, 1)
    GATB = _CV(CSTF, 513, 1)
    M0B = _CV(CSTB, 0, 512)
    IDB = IDBT

    dma("gpsimd", CSTF[:], cstf[:, :], "LDC")
    dma("gpsimd", CSTB[:], cstb[:, 0:512], "LDC")
    ld_consts = dma("gpsimd", IDBT[:], cstb[:, 512:640], "LDC")
    for t in range(TN):
        dma("sync", X[t][:], xin[:, 1024 * t:1024 * t + 1024], f"LDX{t // 4 + 1}".replace("LDX1", "LDX"))
    ldx_tot = [4 * 16] * 4

    dma("gpsimd", CBTB[:], cbt[:, :], "WCC")
    wt_ccin = dma("gpsimd", CUBB[:], cuT[:, :], "WCC")
    for gname in ("LDX", "LDX2", "LDX3", "LDX4"):
        w("gpsimd", gname, 64)
    wt_wd = dma("gpsimd", WDTB[:], wdt[:, :], "WWD")

    op("vector", lambda e: e.memset(GAPT[:], 0.0))
    op("vector", lambda e: e.memset(Z128[:], 0.0))
    op("vector", lambda e: e.memset(EPSc[:], 1e-5))
    dv_init = op("vector", lambda e: e.memset(Srowb[:], 0.0), "DV")

    def finish_debug(*inner_stacks, dump=()):
        w("sync", "DV", C["DV"])
        w("sync", "SC", C["SC"])
        w("sync", "PE", C["PE"])
        for snm in ("WCC", "WWD", "WG0", "WG1"):
            w("sync", snm, C[snm])
        for t in range(TN):
            dma("sync", out[t, :, :], X[t][:], "ST")
        for dst, tiles in dump:
            for i, tile in enumerate(tiles):
                dma("sync", dst[i, :, :], tile[:], "ST")
        w("sync", "ST", C["ST"])
        with nc.Block() as block:
            block.sync(lambda e: [f(e) for f in plan["sync"]] and None)
            block.gpsimd(lambda e: [f(e) for f in plan["gpsimd"]] and None)
            block.vector(lambda e: [f(e) for f in plan["vector"]] and None)
            block.scalar(lambda e: [f(e) for f in plan["scalar"]] and None)
            block.tensor(lambda e: [f(e) for f in plan["tensor"]] and None)
        for st in inner_stacks:
            st.close()
        es.close()
        return nc

    if phases <= 0.25:
        return finish_debug(a2, at)

    w("scalar", "LDC", ld_consts)
    w("scalar", "DV", dv_init)
    sc_sig = op("scalar", lambda e: e.activation(SIGc[:], DECB[:], AF.Sigmoid,
                                                 bias=Z128[:], scale=1.0), "SC")
    w("vector", "SC", sc_sig)
    dv_sig = op("vector", lambda e: e.tensor_scalar(SIGc[:], SIGc[:], 1e-38, None, ALU.max), "DV")
    w("scalar", "DV", dv_sig)
    scv = op("scalar", lambda e: e.activation(LNDc[:], SIGc[:], AF.Ln, bias=Z128[:], scale=1.0), "SC")
    w("scalar", "SC", scv)
    op("scalar", lambda e: e.activation(L0F[:], E0M[:], AF.Exp, bias=Z128[:], scale=LNDc[:]))
    op("scalar", lambda e: e.activation(DPb[:], E0M[0:1, :], AF.Exp,
                                        bias=LNDc[0:1, :], scale=LNDc[0:1, :]))
    op("scalar", lambda e: e.activation(D512[:], E0M[0:1, 511:512], AF.Exp,
                                        bias=LNDc[0:1, :], scale=LNDc[0:1, :]))
    sc_lnd = 0
    for sg in range(4):
        j0 = 511 - 128 * sg
        sc_lnd = op("scalar", lambda e, sg=sg, j0=j0:
                    e.activation(WCf[:, sg:sg + 1], E0M[:, j0:j0 + 1], AF.Exp,
                                 bias=Z128[:], scale=LNDc[:]), "SC")

    w("vector", "SC", sc_lnd)
    op("vector", lambda e: e.tensor_tensor(L0B[:], L0F[:], M0B[:], ALU.mult))
    dv_lprep = op("vector", lambda e: e.tensor_copy(WCb[:], WCf[:]), "DV")

    if phases <= 0.5:
        return finish_debug(a2, at)

    pa = ExitStack()
    CP = [pa.enter_context(nc.psum_tensor(f"cp{i}", [128, 512], F32)) for i in range(4)]

    w("tensor", "WCC", wt_ccin)
    pe_cc = [0] * 16
    dv_ccevac = [0] * 16
    stat_fin = [0] * TN
    agg_fin = [0] * TN
    last_chain_end = {0: 0, 1: 0}
    for g in range(16):
        dj, fc = g // 2, g % 2
        if g >= 4:
            w("tensor", "SC", dv_ccevac[g - 4])
        for k in range(ND):
            pe_cc[g] = op("tensor", lambda e, dj=dj, fc=fc, k=k, g=g:
                          e.matmul(CP[g % 4][:], CBTB[:, 1024 * k + 128 * dj:1024 * k + 128 * dj + 128],
                                   CUBB[:, 1024 * k + 512 * fc:1024 * k + 512 * fc + 512],
                                   start=(k == 0), stop=(k == ND - 1)),
                          "PE" if k == ND - 1 else None)
        w("scalar", "PE", pe_cc[g])
        dv_ccevac[g] = op("scalar", lambda e, dj=dj, fc=fc, g=g:
                          e.activation(HC[dj][:, 512 * fc:512 * fc + 512], CP[g % 4][:],
                                       AF.Identity, bias=0.0, scale=1.0), "SC")
        if phases <= 0.77:
            continue
        t = g
        if t % 4 == 0:
            w("vector", f"LDX{t // 4 + 1}".replace("LDX1", "LDX"), 64)
        p = 12 * (t % 2)
        if t >= 2:
            w("vector", "DV", agg_fin[t - 2])
        op("vector", lambda e, t=t, p=p: e.bn_stats(STA[:, p:p + 6], X[t][:, 0:512]))
        stat_fin[t] = op("vector", lambda e, t=t, p=p: e.bn_stats(STA[:, p + 6:p + 12], X[t][:, 512:1024]), "DV")
        if t >= 1:
            q = 12 * ((t - 1) % 2)
            w("vector", "DV", stat_fin[t - 1])
            op("vector", lambda e, t=t, q=q: e.bn_aggr(aggr_out(MV1, t - 1), STA[:, q:q + 6]))
            gap("vector", 64)
            agg_fin[t - 1] = op("vector", lambda e, t=t, q=q: e.bn_aggr(aggr_out(MV2, t - 1), STA[:, q + 6:q + 12]), "DV")
    if phases <= 0.8:
        return finish_debug(pa, a2, at)
    w("vector", "DV", stat_fin[15])
    q = 12 * (15 % 2)
    op("vector", lambda e, q=q: e.bn_aggr(aggr_out(MV1, 15), STA[:, q:q + 6]))
    gap("vector", 64)
    fin = op("vector", lambda e, q=q: e.bn_aggr(aggr_out(MV2, 15), STA[:, q + 6:q + 12]), "DV")
    w("vector", "DV", fin)

    def ln_combine(n0, n, Mdst, Rdst, NBdst, fold=1.0):
        ms, ds, vs_, d2, vh = WS[0], WS[1], WS[2], WS[3], WS[4]
        sl = lambda T: T[:, n0:n0 + n]
        op("vector", lambda e: e.tensor_tensor(sl(ms), mv_m(MV1, n0, n), mv_m(MV2, n0, n), ALU.add))
        op("vector", lambda e: e.tensor_tensor(sl(ds), mv_m(MV1, n0, n), mv_m(MV2, n0, n), ALU.subtract))
        v = op("vector", lambda e: e.tensor_tensor(sl(vs_), mv_v(MV1, n0, n), mv_v(MV2, n0, n), ALU.add), "DV")
        w("vector", "DV", v)
        op("vector", lambda e: e.tensor_tensor(sl(d2), sl(ds), sl(ds), ALU.mult))
        op("vector", lambda e: e.tensor_scalar(sl(vh), sl(vs_), 0.5, None, ALU.mult))
        v = op("vector", lambda e: e.tensor_scalar(sl(Mdst), sl(ms), 0.5, None, ALU.mult), "DV")
        w("vector", "DV", v)
        dv = op("vector", lambda e: e.scalar_tensor_tensor(sl(vs_), sl(d2), 0.25, sl(vh),
                                                           ALU.mult, ALU.add), "DV")
        w("scalar", "DV", dv)
        sc = op("scalar", lambda e: e.activation(sl(WS[5]), sl(vs_), AF.Sqrt,
                                                 bias=EPSc[:], scale=1.0), "SC")
        w("vector", "SC", sc)
        v = op("vector", lambda e: e.reciprocal(sl(Rdst), sl(WS[5])), "DV")
        w("vector", "DV", v)
        if fold != 1.0:
            v = op("vector", lambda e: e.tensor_scalar(sl(Rdst), sl(Rdst), fold, None, ALU.mult), "DV")
            w("vector", "DV", v)
        return op("vector", lambda e: e.scalar_tensor_tensor(sl(NBdst), sl(Mdst), -1.0, sl(Rdst),
                                                             ALU.mult, ALU.mult), "DV")

    if phases <= 0.85:
        return finish_debug(pa, a2, at)

    dv_ln1 = ln_combine(0, 16, M1c, R1c, NB1c)

    gap("vector")
    dv_hcg = 0
    w("vector", "LDC", ld_consts)
    w("vector", "SC", dv_ccevac[15])
    for j in range(ND):
        dv_hcg = op("vector", lambda e, j=j:
                    e.tensor_scalar(HC[j][:], HC[j][:], GATB[:], None, ALU.mult), "DV")

    if phases <= 0.9:
        return finish_debug(pa, a2, at)

    sc_x1 = [0] * TN

    def emit_x1_norm(t):
        w("vector", "DV", dv_ln1)
        sc_x1[t] = op("vector", lambda e, t=t:
                      e.tensor_scalar(XW[t][:], X[t][:], R1c[:, t:t + 1], NB1c[:, t:t + 1],
                                      ALU.mult, ALU.add), "DV")

    for t in range(4):
        emit_x1_norm(t)

    pa.close()
    pb = ExitStack()
    PB = [pb.enter_context(nc.psum_tensor(f"pb{i}", [128, 512], F32)) for i in range(8)]

    pe_scan = [[0] * 8 for _ in range(NCH)]
    pe_mv = [0] * NCH
    dv_ztevac = [[0] * 8 for _ in range(NCH)]
    dv_state = [0] * NCH

    if phases <= 1:
        return finish_debug(pa, a2, at)

    w("tensor", "DV", dv_lprep)
    w("tensor", "SC", dv_ccevac[15])

    a2.close()
    DUMP_TILES = (0, 15)
    STDMP = [at.enter_context(nc.sbuf_tensor(f"stdmp{t}", [128, 8], F32)) for t in range(2)] if phases < 90 else None
    ZT = [at.enter_context(nc.sbuf_tensor(f"zt{d}", [128, 2048], BF16)) for d in range(ND)]
    T1A = [at.enter_context(nc.sbuf_tensor(f"t1a{i}", [128, 1024], F32)) for i in range(2)]
    SQD1 = at.enter_context(nc.sbuf_tensor("sqd", [128, 1024], BF16))
    SQD = [SQD1, SQD1]

    w("scalar", "DV", dv_hcg)
    for I in range(NCH):
        w("tensor", "DV", sc_x1[4 * I + 3])
        if I > 0:
            w("tensor", "DV", dv_state[I - 1])
        for d in range(ND):
            if I > 0:
                w("tensor", "SC", dv_ztevac[I - 1][d])
                op("tensor", lambda e, d=d:
                   e.matmul(PB[d][:, 0:512], Srowb[0:1, 128 * d:128 * d + 128], DPb[0:1, :],
                            start=True, stop=False, skip_group_check=True))
            for sg in range(4):
                last = (sg == 3)
                pe_scan[I][d] = op("tensor", lambda e, I=I, d=d, sg=sg:
                                   e.matmul(PB[d][:, 128 * sg:512],
                                            XW[4 * I + sg][:, 128 * d:128 * d + 128],
                                            L0B[:, 0:512 - 128 * sg],
                                            start=(I == 0 and sg == 0), stop=(sg == 3),
                                            skip_group_check=True),
                                   "PE" if last else None)
        for d in range(2):
            w("scalar", "PE", pe_scan[I][d])
            dv_ztevac[I][d] = op("scalar", lambda e, I=I, d=d:
                                 e.activation(ZT[d][:, 512 * I:512 * I + 512], PB[d][:, 0:512],
                                              AF.Identity, bias=0.0, scale=1.0), "SC")
        if I < NCH - 1:
            w("tensor", "SC", dv_ztevac[I][1])
            for dc in range(2):
                for sg in range(4):
                    pe_mv[I] = op("tensor", lambda e, I=I, dc=dc, sg=sg:
                                  e.matmul(PB[dc][0:1, 0:512], WCb[:, sg:sg + 1],
                                           XW[4 * I + sg][:, 512 * dc:512 * dc + 512],
                                           start=(sg == 0), stop=(sg == 3),
                                           skip_group_check=True),
                                  "PE" if (sg == 3) else None)
        for d in range(2, ND):
            w("scalar", "PE", pe_scan[I][d])
            dv_ztevac[I][d] = op("scalar", lambda e, I=I, d=d:
                                 e.activation(ZT[d][:, 512 * I:512 * I + 512], PB[d][:, 0:512],
                                              AF.Identity, bias=0.0, scale=1.0), "SC")
        if I < NCH - 1:
            for t_ in range(4 * (I + 1), 4 * (I + 2)):
                emit_x1_norm(t_)
        if I < NCH - 1:
            w("vector", "PE", pe_mv[I])
            w("vector", "DV", dv_init if I == 0 else dv_state[I - 1])
            op("vector", lambda e: e.scalar_tensor_tensor(
                Srowb[0:1, 0:512], Srowb[0:1, 0:512], D512[0:1, 0:1],
                PB[0][0:1, 0:512], ALU.mult, ALU.add))
            dv_state[I] = op("vector", lambda e: e.scalar_tensor_tensor(
                Srowb[0:1, 512:1024], Srowb[0:1, 512:1024], D512[0:1, 0:1],
                PB[1][0:1, 0:512], ALU.mult, ALU.add), "DV")

    if phases <= 2:
        return finish_debug(pb, at)

    INVD = 1.0 / 1024.0

    def lnacc_s1(PSRC, Tb, c, wait_pe, t_free_dv):
        w("scalar", "PE", wait_pe)
        w("scalar", "DV", t_free_dv)
        return op("scalar", lambda e: e.activation(
            Tb[:], PSRC[:], AF.Identity, bias=0.0, scale=1.0,
            accum_out=SCR[:, c:c + 1]), "SC")

    def lnacc_s2(Tb, SQb, c, sc_cp):
        w("vector", "SC", sc_cp)
        op("vector", lambda e: e.tensor_scalar(SCR[:, c:c + 1], SCR[:, c:c + 1], INVD, None, ALU.mult))
        v = op("vector", lambda e: e.scalar_tensor_tensor(
            SQb[:], Tb[:], 1.0, Tb[:], ALU.mult, ALU.mult, accum_out=SCR[:, c + 1:c + 2]), "DV")
        w("vector", "DV", v)
        v = op("vector", lambda e: e.scalar_tensor_tensor(
            SCR[:, c + 2:c + 3], SCR[:, c:c + 1], 1.0, SCR[:, c:c + 1], ALU.mult, ALU.mult), "DV")
        w("vector", "DV", v)
        dv_v = op("vector", lambda e: e.tensor_scalar(
            SCR[:, c + 3:c + 4], SCR[:, c + 1:c + 2], INVD, SCR[:, c + 2:c + 3],
            ALU.mult, ALU.subtract), "DV")
        w("scalar", "DV", dv_v)
        sc_sq = op("scalar", lambda e: e.activation(
            SCR[:, c + 4:c + 5], SCR[:, c + 3:c + 4], AF.Sqrt, bias=EPSc[:], scale=1.0), "SC")
        return sc_sq

    def lnacc_s3(Tb, c, Xt, sc_sq, x_war_dv):
        w("vector", "SC", sc_sq)
        v = op("vector", lambda e: e.reciprocal(SCR[:, c + 4:c + 5], SCR[:, c + 4:c + 5]), "DV")
        w("vector", "DV", v)
        dv_nb = op("vector", lambda e: e.scalar_tensor_tensor(
            SCR[:, c + 5:c + 6], SCR[:, c:c + 1], -1.0, SCR[:, c + 4:c + 5], ALU.mult, ALU.mult), "DV")
        w("scalar", "DV", dv_nb)
        sc_nrm = op("scalar", lambda e: e.activation(
            Tb[:], Tb[:], AF.Identity, bias=SCR[:, c + 5:c + 6], scale=SCR[:, c + 4:c + 5]), "SC")
        w("vector", "SC", sc_nrm)
        w("vector", "DV", x_war_dv)
        dv_xadd = op("vector", lambda e: e.tensor_tensor(Xt[:], Xt[:], Tb[:], ALU.add), "DV")
        return sc_nrm, dv_xadd

    pb.close()
    pc = ExitStack()
    PR = [pc.enter_context(nc.psum_tensor(f"pr{i}", [128, 1024], F32)) for i in range(4)]

    pe_retr = [0] * TN
    sc_cp = [0] * TN
    sc_sqa = [0] * TN
    dv_nrm_a = [0] * TN
    dv_xattn = [0] * TN

    def attn_stage1(t):
        p, pr = t % 4, t % 2
        return lnacc_s1(PR[p], T1A[pr], 6 * pr,
                        wait_pe=pe_retr[t],
                        t_free_dv=dv_xattn[t - 2] if t >= 2 else 0)

    def attn_stage2(t):
        pr = t % 2
        return lnacc_s2(T1A[pr], SQD[pr], 6 * pr, sc_cp[t])

    def attn_stage3(t):
        pr = t % 2
        nrm, xadd = lnacc_s3(T1A[pr], 6 * pr, X[t], sc_sqa[t], 0)
        dv_nrm_a[t] = nrm
        dv_xattn[t] = xadd

    for t in range(TN):
        p = t % 4
        w("tensor", "SC", dv_ztevac[NCH - 1][7])
        w("tensor", "DV", dv_hcg)
        if t >= 4:
            w("tensor", "SC", sc_cp[t - 4])
        for fc in range(2):
            for d in range(ND):
                op("tensor", lambda e, t=t, p=p, fc=fc, d=d:
                   e.matmul(PR[p][:, 512 * fc:512 * fc + 512],
                            ZT[d][:, 128 * t:128 * t + 128],
                            HC[d][:, 512 * fc:512 * fc + 512],
                            start=(d == 0), stop=False, skip_group_check=True))
            pe_retr[t] = op("tensor", lambda e, t=t, p=p, fc=fc:
                            e.matmul(PR[p][:, 512 * fc:512 * fc + 512], IDB[:],
                                     XW[t][:, 512 * fc:512 * fc + 512],
                                     start=False, stop=True, skip_group_check=True), "PE")
        if t >= 2:
            attn_stage3(t - 2)
            tl = t - 2
            if tl <= 7:
                p2 = 12 * (tl % 2)
                w("vector", "DV", dv_xattn[tl])
                if tl >= 2:
                    w("vector", "DV", agg_fin[tl - 2])
                op("vector", lambda e, tl=tl, p2=p2: e.bn_stats(STA[:, p2:p2 + 6], X[tl][:, 0:512]))
                stat_fin[tl] = op("vector", lambda e, tl=tl, p2=p2:
                                  e.bn_stats(STA[:, p2 + 6:p2 + 12], X[tl][:, 512:1024]), "DV")
                if tl >= 1:
                    q2 = 12 * ((tl - 1) % 2)
                    w("vector", "DV", stat_fin[tl - 1])
                    op("vector", lambda e, tl=tl, q2=q2: e.bn_aggr(aggr_out(MV1, tl - 1), STA[:, q2:q2 + 6]))
                    gap("vector", 64)
                    agg_fin[tl - 1] = op("vector", lambda e, tl=tl, q2=q2:
                                         e.bn_aggr(aggr_out(MV2, tl - 1), STA[:, q2 + 6:q2 + 12]), "DV")
        if t >= 1:
            sc_sqa[t - 1] = attn_stage2(t - 1)
        sc_cp[t] = attn_stage1(t)
    sc_sqa[TN - 1] = attn_stage2(TN - 1)
    attn_stage3(TN - 2)
    attn_stage3(TN - 1)
    w("vector", "DV", stat_fin[7])
    q2 = 12 * (7 % 2)
    op("vector", lambda e, q2=q2: e.bn_aggr(aggr_out(MV1, 7), STA[:, q2:q2 + 6]))
    gap("vector", 64)
    fin2 = op("vector", lambda e, q2=q2: e.bn_aggr(aggr_out(MV2, 7), STA[:, q2 + 6:q2 + 12]), "DV")
    w("vector", "DV", fin2)
    dv_ln2a_early = ln_combine(0, 8, M1c, R2s, NB2s, fold=INV127)
    if phases < 90:
        for i, t in enumerate(DUMP_TILES):
            w("vector", "DV", dv_xattn[t])
            op("vector", lambda e, i=i, c=6 * (t % 2): e.tensor_copy(STDMP[i][:, 0:6], SCR[:, c:c + 6]))

    if phases <= 3:
        return finish_debug(pc, at, dump=[(zt_o, ZT), (hc_o, HC), (xw_o, XW), (st_o, STDMP)])

    ffps_pr0 = PR[0]
    pc.close()
    at.close()

    ff = ExitStack()
    fsb = lambda n, shp, dt: ff.enter_context(nc.sbuf_tensor(n, shp, dt))
    X2T = fsb("x2t", [128, 8192], BF16)
    X2S = [fsb(f"x2s{i}", [128, 1024], BF16) for i in range(2)]
    SIL = [fsb(f"sil{i}", [128, 1024], BF16) for i in range(2)]
    T1FA = [fsb(f"t1f{i}", [128, 1024], F32) for i in range(2)]
    SQDF1 = fsb("sqdf", [128, 1024], BF16)
    SQDF = [SQDF1, SQDF1]
    WGU = [fsb(f"wgu{i}", [128, 2048], BF16) for i in range(2)]

    ffps = {"PR0": ffps_pr0}
    prev_dn_sc = 0
    slot_pe = [0, 0]
    prev_half_sc = sc_cp[15]
    prev_half_dv = dv_xattn[15]

    w("gpsimd", "PE", pe_retr[15])
    w("gpsimd", "DV", dv_xattn[15])
    w("gpsimd", "SC", sc_cp[15])
    w("scalar", "PE", pe_retr[15])
    w("scalar", "DV", dv_xattn[15])
    w("vector", "PE", pe_retr[15])
    w("vector", "SC", sc_cp[15])
    dv_xfin = [0] * TN

    def emit_ln2(hf, war_dv):
        tg0 = 8 * hf
        w("vector", "DV", war_dv)
        for tl in range(8):
            t = tg0 + tl
            p = 12 * (tl % 2)
            w("vector", "DV", dv_xattn[t])
            if tl >= 2:
                w("vector", "DV", agg_fin[tl - 2])
            op("vector", lambda e, t=t, p=p: e.bn_stats(STA[:, p:p + 6], X[t][:, 0:512]))
            stat_fin[tl] = op("vector", lambda e, t=t, p=p: e.bn_stats(STA[:, p + 6:p + 12], X[t][:, 512:1024]), "DV")
            if tl >= 1:
                q = 12 * ((tl - 1) % 2)
                w("vector", "DV", stat_fin[tl - 1])
                op("vector", lambda e, t=t - 1, q=q: e.bn_aggr(aggr_out(MV1, t), STA[:, q:q + 6]))
                gap("vector", 64)
                agg_fin[tl - 1] = op("vector", lambda e, t=t - 1, q=q: e.bn_aggr(aggr_out(MV2, t), STA[:, q + 6:q + 12]), "DV")
        w("vector", "DV", stat_fin[7])
        q = 12 * (7 % 2)
        op("vector", lambda e, t=tg0 + 7, q=q: e.bn_aggr(aggr_out(MV1, t), STA[:, q:q + 6]))
        gap("vector", 64)
        fin = op("vector", lambda e, t=tg0 + 7, q=q: e.bn_aggr(aggr_out(MV2, t), STA[:, q + 6:q + 12]), "DV")
        w("vector", "DV", fin)
        return ln_combine(tg0, 8, M1c, R2s, NB2s, fold=INV127)

    def emit_x2pipe(hf, dv_ln2, psum_sc, psum_dv, prev_pe_tr):
        tg0 = 8 * hf
        PT = ffps["PT"]
        sc_x2 = [0] * 8
        pe_tr = [0] * 8
        dv_pt = [0] * 8
        for tl in range(8):
            t = tg0 + tl
            w("scalar", "DV", dv_ln2)
            w("scalar", "PE", pe_tr[tl - 2] if tl >= 2 else prev_pe_tr[tl - 2])
            sc_x2[tl] = op("scalar", lambda e, t=t, tl=tl: e.activation(
                X2S[tl % 2][:], X[t][:], AF.Identity,
                bias=NB2s[:, t:t + 1], scale=R2s[:, t:t + 1]), "SC")
            w("tensor", "SC", sc_x2[tl])
            if tl == 0:
                w("tensor", "SC", psum_sc)
                w("tensor", "DV", psum_dv)
            else:
                w("tensor", "DV", dv_pt[tl - 1])
            for d in range(ND):
                pe_tr[tl] = op("tensor", lambda e, tl=tl, d=d: e.transpose(
                    PT[:, 128 * d:128 * d + 128], X2S[tl % 2][:, 128 * d:128 * d + 128], IDB[:]),
                    "PE" if d == ND - 1 else None)
            w("vector", "PE", pe_tr[tl])
            dv_pt[tl] = op("vector", lambda e, tl=tl: e.tensor_copy(
                X2T[...].rearrange("p (d t) -> p d t", d=ND)[:, :, 128 * tl:128 * tl + 128],
                PT[...].rearrange("p (d t) -> p d t", d=ND)[:, :, :]), "DV")
        return sc_x2, pe_tr, dv_pt

    pt_scope = ExitStack()
    ffps["PT"] = pt_scope.enter_context(nc.psum_tensor("pt", [128, 1024], BF16))
    pt_scope.close()

    dv_ln2_done = 0
    x2pipe_B = None

    for hf in range(2):
        tg0 = 8 * hf

        if hf == 0:
            dv_ln2_done = emit_ln2(0, 0)
            for wv in (stat_fin[1], stat_fin[3], stat_fin[5], stat_fin[7],
                       agg_fin[6], dv_ln2_done):
                w("tensor", "DV", wv)
                op("tensor", lambda e: e.matmul(
                    ffps["PR0"][0:64, 0:64], XW[0][:, 0:64], XW[0][:, 0:64],
                    start=True, stop=True, skip_group_check=True))
            sc_x2, pe_tr, dv_pt = emit_x2pipe(0, dv_ln2_done, prev_half_sc, prev_half_dv,
                                              [pe_retr[15]] * 2)
        else:
            sc_x2, pe_tr, dv_pt = x2pipe_B
        if phases <= 4:
            return finish_debug(ff)

        if hf == 0:
            pg_scope = ExitStack()
            PG = [pg_scope.enter_context(nc.psum_tensor(f"pg{i}", [128, 1024], F32)) for i in range(2)]
            PU = [pg_scope.enter_context(nc.psum_tensor(f"pu{i}", [128, 1024], F32)) for i in range(2)]
            ffps["PG"], ffps["PU"] = PG, PU
        else:
            PG, PU = ffps["PG"], ffps["PU"]
        wt_gu = [0] * NH
        pe_g = [[0] * 2 for _ in range(NH)]
        pe_u = [[0] * 2 for _ in range(NH)]
        sc_sil = [[0] * 2 for _ in range(NH)]
        dv_ht = [[0] * 2 for _ in range(NH)]

        def HT(eta):
            return XW[eta] if eta < 16 else HC[eta - 16]

        for eta in range(NH):
            sl = eta % 2
            if hf == 1 and eta == 0:
                w("tensor", "SC", prev_dn_sc)
            w("gpsimd", "PE", pe_u[eta - 2][1] if eta >= 2 else slot_pe[sl])
            wt_gu[eta] = dma("gpsimd", WGU[sl][:], wgu[eta, :, :], f"WG{sl}")
            w("tensor", f"WG{sl}", wt_gu[eta])
            w("tensor", "DV", dv_pt[7])
            if eta >= 2:
                w("tensor", "SC", sc_sil[eta - 2][1])
                w("tensor", "DV", dv_ht[eta - 2][1])
            for c in range(2):
                for d in range(ND):
                    pe_g[eta][c] = op("tensor", lambda e, sl=sl, c=c, d=d: e.matmul(
                        PG[sl][:, 512 * c:512 * c + 512],
                        WGU[sl][:, 128 * d:128 * d + 128],
                        X2T[:, 1024 * d + 512 * c:1024 * d + 512 * c + 512],
                        start=(d == 0), stop=(d == ND - 1)),
                        "PE" if d == ND - 1 else None)
            for c in range(2):
                for d in range(ND):
                    pe_u[eta][c] = op("tensor", lambda e, sl=sl, c=c, d=d: e.matmul(
                        PU[sl][:, 512 * c:512 * c + 512],
                        WGU[sl][:, 1024 + 128 * d:1024 + 128 * d + 128],
                        X2T[:, 1024 * d + 512 * c:1024 * d + 512 * c + 512],
                        start=(d == 0), stop=(d == ND - 1)),
                        "PE" if d == ND - 1 else None)
            slot_pe[sl] = pe_u[eta][1]
            if hf == 0 and eta == 6:
                dv_ln2_done = emit_ln2(1, dv_ln2_done)
            for c in range(2):
                w("scalar", "PE", pe_g[eta][c])
                if eta >= 2:
                    w("scalar", "DV", dv_ht[eta - 2][c])
                sc_sil[eta][c] = op("scalar", lambda e, sl=sl, c=c: e.activation(
                    SIL[sl][:, 512 * c:512 * c + 512], PG[sl][:, 512 * c:512 * c + 512],
                    AF.Silu, bias=Z128[:], scale=1.0), "SC")
            for c in range(2):
                w("vector", "SC", sc_sil[eta][c])
                w("vector", "PE", pe_u[eta][c])
                dv_ht[eta][c] = op("vector", lambda e, eta=eta, sl=sl, c=c: e.scalar_tensor_tensor(
                    HT(eta)[:, 512 * c:512 * c + 512], SIL[sl][:, 512 * c:512 * c + 512],
                    INV127, PU[sl][:, 512 * c:512 * c + 512], ALU.mult, ALU.mult), "DV")
        if hf == 0:
            pg_scope.close()
        if phases <= 5:
            return finish_debug(ff)

        if hf == 0:
            pf_scope = ExitStack()
            PF = [pf_scope.enter_context(nc.psum_tensor(f"pf{i}", [128, 1024], F32)) for i in range(2)]
            ffps["PF"] = PF
        else:
            PF = ffps["PF"]
        PFS = [PF[0], PF[1], PU[0], PU[1]]
        pe_dn = [0] * 8
        sc_cpf = [0] * 8
        sc_sqf = [0] * 8
        dv_nrmf = [0] * 8

        def dn_stage1(tl):
            pr = tl % 2
            return lnacc_s1(PFS[tl % 4], T1FA[pr], 6 * pr,
                            wait_pe=pe_dn[tl],
                            t_free_dv=dv_xfin[tg0 + tl - 2] if tl >= 2 else
                            (dv_xfin[7] if hf == 1 else 0))

        def dn_stage2(tl):
            pr = tl % 2
            return lnacc_s2(T1FA[pr], SQDF[pr], 6 * pr, sc_cpf[tl])

        def dn_stage3(tl):
            t = tg0 + tl
            pr = tl % 2
            nrm, xadd = lnacc_s3(T1FA[pr], 6 * pr, X[t], sc_sqf[tl], dv_xattn[t])
            dv_nrmf[tl] = nrm
            dv_xfin[t] = xadd
            w("sync", "DV", dv_xfin[t])
            dma("sync", out[t, :, :], X[t][:], "ST")

        w("tensor", "WWD", wt_wd)
        w("tensor", "DV", dv_ht[NH - 1][1])
        w("tensor", "SC", sc_sil[NH - 1][1])
        for tl in range(8):
            t = tg0 + tl
            sl = tl % 4
            if tl >= 4:
                w("tensor", "SC", sc_cpf[tl - 4])
            for fc in range(2):
                for eta in range(NH):
                    pe_dn[tl] = op("tensor", lambda e, sl=sl, tl=tl, fc=fc, eta=eta: e.matmul(
                        PFS[sl][:, 512 * fc:512 * fc + 512],
                        HT(eta)[:, 128 * tl:128 * tl + 128],
                        WDTB[:, 1024 * eta + 512 * fc:1024 * eta + 512 * fc + 512],
                        start=(eta == 0), stop=(eta == NH - 1)),
                        "PE" if (eta == NH - 1 and fc == 1) else None)
            if tl >= 2:
                dn_stage3(tl - 2)
            if tl >= 1:
                sc_sqf[tl - 1] = dn_stage2(tl - 1)
            sc_cpf[tl] = dn_stage1(tl)
            if hf == 0 and tl == 5:
                x2pipe_B = emit_x2pipe(1, dv_ln2_done, sc_cpf[4], 0, pe_tr)
        sc_sqf[7] = dn_stage2(7)
        dn_stage3(6)
        dn_stage3(7)
        if hf == 0:
            pf_scope.close()
        if phases <= 6 and hf == 0:
            return finish_debug(ff)
        prev_dn_sc = sc_cpf[7]

    ff.close()

    w("sync", "ST", C["ST"])

    with nc.Block() as block:
        @block.sync
        def _(e):
            for f in plan["sync"]:
                f(e)

        @block.gpsimd
        def _(e):
            for f in plan["gpsimd"]:
                f(e)

        @block.vector
        def _(e):
            for f in plan["vector"]:
                f(e)

        @block.scalar
        def _(e):
            for f in plan["scalar"]:
                f(e)

        @block.tensor
        def _(e):
            for f in plan["tensor"]:
                f(e)

    es.close()
    return nc


def kernel(L0_attn_x, L0_attn_bind_key, L0_attn_unbind_key, L0_attn_gate, L0_attn_decay,
           L0_ffn_w_gate, L0_ffn_w_up, L0_ffn_w_down):
    x = np.ascontiguousarray(np.asarray(L0_attn_x), dtype=np.float32)
    b = np.asarray(L0_attn_bind_key)
    u = np.asarray(L0_attn_unbind_key)
    gate = np.asarray(L0_attn_gate).reshape(1)
    decay = np.asarray(L0_attn_decay).reshape(())
    wg = np.asarray(L0_ffn_w_gate)
    wu = np.asarray(L0_ffn_w_up)
    wd = np.asarray(L0_ffn_w_down)

    D = 1024
    idx = np.arange(D)
    circ = (idx[:, None] - idx[None, :]) % D
    cbt = np.ascontiguousarray(b[circ].reshape(ND, 128, D).transpose(1, 0, 2).reshape(128, ND * D))
    cuT = np.ascontiguousarray(u[circ].reshape(ND, 128, D).transpose(1, 0, 2).reshape(128, ND * D))

    ii = np.arange(128)[:, None]
    jj = np.arange(512)[None, :]
    e0m = np.maximum(jj - ii, 0).astype(np.float32)
    cstf = np.ascontiguousarray(np.concatenate(
        [e0m, np.broadcast_to(decay, (128, 1)).astype(np.float32),
         np.broadcast_to(gate, (128, 1)).astype(np.float32)], axis=1))
    cstb = np.ascontiguousarray(np.concatenate(
        [(jj >= ii).astype(ml_dtypes.bfloat16), np.eye(128, dtype=ml_dtypes.bfloat16)], axis=1))

    wgt = wg.reshape(NH, 128, ND, 128).transpose(0, 3, 2, 1).reshape(NH, 128, 1024)
    wut = wu.reshape(NH, 128, ND, 128).transpose(0, 3, 2, 1).reshape(NH, 128, 1024)
    wgu_h = np.ascontiguousarray(np.concatenate([wgt, wut], axis=2))
    wdt = np.ascontiguousarray(wd.T.reshape(NH, 128, 1024).transpose(1, 0, 2).reshape(128, NH * 1024))

    nc = build_nc()
    in_maps = []
    for c in range(8):
        in_maps.append(dict(
            xin=np.ascontiguousarray(x[c].reshape(TN, 128, 1024).transpose(1, 0, 2).reshape(128, TN * 1024)),
            cbt=cbt, cuT=cuT, cstf=cstf, cstb=cstb,
            wgu=wgu_h, wdt=wdt,
        ))
    res = run_bass_kernel_spmd(nc, in_maps, core_ids=list(range(8)))
    return np.stack([res.results[c]["out"].reshape(2048, 1024) for c in range(8)], axis=0)


# version 19
# speedup vs baseline: 1.0246x; 1.0246x over previous
import numpy as np
import ml_dtypes
from contextlib import ExitStack

import concourse.bass as bass
import concourse.mybir as mybir
from concourse.bass_utils import run_bass_kernel_spmd

F32 = mybir.dt.float32
BF16 = mybir.dt.bfloat16
I32 = mybir.dt.int32
AF = mybir.ActivationFunctionType
ALU = mybir.AluOpType

TN = 16
ND = 8
NH = 24
NCH = 4
INV127 = 1.0 / 127.0


def build_nc(phases=99):
    nc = bass.Bass("TRN2", target_bir_lowering=False)

    xin = nc.dram_tensor("xin", [128, TN * 1024], F32, kind="ExternalInput")
    cbt = nc.dram_tensor("cbt", [128, ND * 1024], F32, kind="ExternalInput")
    cuT = nc.dram_tensor("cuT", [128, ND * 1024], F32, kind="ExternalInput")
    cstf = nc.dram_tensor("cstf", [128, 514], F32, kind="ExternalInput")
    cstb = nc.dram_tensor("cstb", [128, 640], BF16, kind="ExternalInput")
    wgu = nc.dram_tensor("wgu", [NH, 128, 2048], I32, kind="ExternalInput")
    wdt = nc.dram_tensor("wdt", [128, NH * 1024], I32, kind="ExternalInput")
    out = nc.dram_tensor("out", [TN, 128, 1024], F32, kind="ExternalOutput")
    if phases < 90:
        zt_o = nc.dram_tensor("zt_o", [ND, 128, 2048], BF16, kind="ExternalOutput")
        hc_o = nc.dram_tensor("hc_o", [ND, 128, 1024], BF16, kind="ExternalOutput")
        xw_o = nc.dram_tensor("xw_o", [TN, 128, 1024], BF16, kind="ExternalOutput")
        st_o = nc.dram_tensor("st_o", [2, 128, 8], F32, kind="ExternalOutput")

    plan = {k: [] for k in ("sync", "gpsimd", "vector", "scalar", "tensor")}
    C = {"LDX": 0, "LDX2": 0, "LDX3": 0, "LDX4": 0, "LDC": 0, "WCC": 0, "WWD": 0,
         "WG0": 0, "WG1": 0, "PE": 0, "DV": 0, "SC": 0, "GP": 0, "ST": 0}
    S = {}

    def w(eng, sem, val):
        if val and val > 0:
            plan[eng].append(lambda e, s=sem, v=val: e.wait_ge(S[s], v))

    _gapn = {"vector": 0, "scalar": 0}

    def gap(eng, cyc=300):
        n = max(1, cyc // 200)
        for _ in range(n):
            i = _gapn[eng] % 3
            _gapn[eng] += 1
            if eng == "vector":
                plan[eng].append(lambda e, i=i: e.tensor_copy(GAPT[:, i:i + 1], Z128[:]))
            else:
                plan[eng].append(lambda e, i=i: e.activation(GAPS[:, i:i + 1], Z128[:],
                                                             AF.Identity, bias=0.0, scale=1.0))

    def dma(eng, out_ap, in_ap, sem):
        C[sem] += 16
        plan[eng].append(lambda e, o=out_ap, i=in_ap, s=sem:
                         e.dma_start(out=o, in_=i).then_inc(S[s], 16))
        return C[sem]

    def op(eng, fn, sem=None):
        if sem is None:
            plan[eng].append(fn)
            return None
        C[sem] += 1
        plan[eng].append(lambda e, f=fn, s=sem: f(e).then_inc(S[s], 1))
        return C[sem]

    es = ExitStack()
    for name in C:
        S[name] = es.enter_context(nc.semaphore(name))

    sb = lambda n, shp, dt: es.enter_context(nc.sbuf_tensor(n, shp, dt))

    X = [sb(f"x{t}", [128, 1024], F32) for t in range(TN)]
    XW = [sb(f"xw{t}", [128, 1024], BF16) for t in range(TN)]
    HC = [sb(f"hc{j}", [128, 1024], BF16) for j in range(ND)]
    WDTB = sb("wdtb", [128, NH * 1024], BF16)
    IDBT = sb("idb_s", [128, 128], BF16)
    M1c = sb("m1c", [128, 16], F32)
    R1c = sb("r1c", [128, 16], F32)
    NB1c = sb("nb1c", [128, 16], F32)
    R2s = sb("r2s", [128, 16], F32)
    NB2s = sb("nb2s", [128, 16], F32)
    STA = sb("sta", [128, 24], F32)
    MV1 = sb("mv1", [128, 32], F32)
    MV2 = sb("mv2", [128, 32], F32)
    WS = [sb(f"ws{i}", [128, 16], F32) for i in range(6)]
    SCR = sb("scr", [128, 12], F32)
    LMV = sb("lmv", [128, 8], F32)
    LSTA = sb("lsta", [128, 24], F32)
    EPSc = sb("epsc", [128, 1], F32)
    Z128 = sb("z128", [128, 1], F32)
    SIGc = sb("sigc", [128, 1], F32)
    LNDc = sb("lndc", [128, 1], F32)
    GAPT = sb("gapt", [128, 3], F32)
    GAPS = sb("gaps", [128, 3], F32)

    def mv_m(MV, lo, n):
        return MV[:, 0 + lo: 0 + lo + n]

    def mv_v(MV, lo, n):
        return MV[:, 16 + lo: 16 + lo + n]

    def aggr_out(MV, t):
        return MV[:, :].rearrange("p (two t) -> p two t", two=2)[:, :, t:t + 1]

    at = ExitStack()
    asb = lambda n, shp, dt: at.enter_context(nc.sbuf_tensor(n, shp, dt))
    L0B = asb("l0b", [128, 512], BF16)
    DPb = asb("dpb", [1, 512], BF16)
    D512 = asb("d512", [1, 1], F32)
    WCf = asb("wcf", [128, 4], F32)
    WCb = asb("wcb", [128, 4], BF16)
    Srowb = asb("srowb", [1, 1024], BF16)

    a2 = ExitStack()
    a2sb = lambda n, shp, dt: a2.enter_context(nc.sbuf_tensor(n, shp, dt))
    CBTB = a2sb("cbtb", [128, ND * 1024], BF16)
    CUBB = a2sb("cubb", [128, ND * 1024], BF16)
    L0F = a2sb("l0f", [128, 512], BF16)
    CSTF = a2sb("cstf_s", [128, 514], F32)
    CSTB = a2sb("cstb_s", [128, 512], BF16)

    class _CV:
        def __init__(self, T, f0, fN):
            self.T, self.f0, self.fN = T, f0, fN

        def __getitem__(self, key):
            if key == slice(None):
                return self.T[:, self.f0:self.f0 + self.fN]
            p, f = key
            fs = self.f0 + (f.start or 0)
            fe = self.f0 + (f.stop if f.stop is not None else self.fN)
            return self.T[p, fs:fe]

    E0M = _CV(CSTF, 0, 512)
    DECB = _CV(CSTF, 512, 1)
    GATB = _CV(CSTF, 513, 1)
    M0B = _CV(CSTB, 0, 512)
    IDB = IDBT

    dma("gpsimd", CSTF[:], cstf[:, :], "LDC")
    dma("gpsimd", CSTB[:], cstb[:, 0:512], "LDC")
    ld_consts = dma("gpsimd", IDBT[:], cstb[:, 512:640], "LDC")
    w("sync", "WCC", 32)
    for t in range(TN):
        dma("sync", X[t][:], xin[:, 1024 * t:1024 * t + 1024], f"LDX{t // 4 + 1}".replace("LDX1", "LDX"))
    ldx_tot = [4 * 16] * 4

    dma("gpsimd", CBTB[:], cbt[:, :], "WCC")
    wt_ccin = dma("gpsimd", CUBB[:], cuT[:, :], "WCC")
    for gname in ("LDX", "LDX2", "LDX3", "LDX4"):
        w("gpsimd", gname, 64)
    wt_wd = dma("gpsimd", WDTB[:], wdt[:, :], "WWD")

    op("vector", lambda e: e.memset(GAPT[:], 0.0))
    op("vector", lambda e: e.memset(Z128[:], 0.0))
    op("vector", lambda e: e.memset(EPSc[:], 1e-5))
    dv_init = op("vector", lambda e: e.memset(Srowb[:], 0.0), "DV")

    def finish_debug(*inner_stacks, dump=()):
        w("sync", "DV", C["DV"])
        w("sync", "SC", C["SC"])
        w("sync", "PE", C["PE"])
        for snm in ("WCC", "WWD", "WG0", "WG1"):
            w("sync", snm, C[snm])
        for t in range(TN):
            dma("sync", out[t, :, :], X[t][:], "ST")
        for dst, tiles in dump:
            for i, tile in enumerate(tiles):
                dma("sync", dst[i, :, :], tile[:], "ST")
        w("sync", "ST", C["ST"])
        with nc.Block() as block:
            block.sync(lambda e: [f(e) for f in plan["sync"]] and None)
            block.gpsimd(lambda e: [f(e) for f in plan["gpsimd"]] and None)
            block.vector(lambda e: [f(e) for f in plan["vector"]] and None)
            block.scalar(lambda e: [f(e) for f in plan["scalar"]] and None)
            block.tensor(lambda e: [f(e) for f in plan["tensor"]] and None)
        for st in inner_stacks:
            st.close()
        es.close()
        return nc

    if phases <= 0.25:
        return finish_debug(a2, at)

    w("scalar", "LDC", ld_consts)
    w("scalar", "DV", dv_init)
    sc_sig = op("scalar", lambda e: e.activation(SIGc[:], DECB[:], AF.Sigmoid,
                                                 bias=Z128[:], scale=1.0), "SC")
    w("vector", "SC", sc_sig)
    dv_sig = op("vector", lambda e: e.tensor_scalar(SIGc[:], SIGc[:], 1e-38, None, ALU.max), "DV")
    w("scalar", "DV", dv_sig)
    scv = op("scalar", lambda e: e.activation(LNDc[:], SIGc[:], AF.Ln, bias=Z128[:], scale=1.0), "SC")
    w("scalar", "SC", scv)
    op("scalar", lambda e: e.activation(L0F[:], E0M[:], AF.Exp, bias=Z128[:], scale=LNDc[:]))
    op("scalar", lambda e: e.activation(DPb[:], E0M[0:1, :], AF.Exp,
                                        bias=LNDc[0:1, :], scale=LNDc[0:1, :]))
    op("scalar", lambda e: e.activation(D512[:], E0M[0:1, 511:512], AF.Exp,
                                        bias=LNDc[0:1, :], scale=LNDc[0:1, :]))
    sc_lnd = 0
    for sg in range(4):
        j0 = 511 - 128 * sg
        sc_lnd = op("scalar", lambda e, sg=sg, j0=j0:
                    e.activation(WCf[:, sg:sg + 1], E0M[:, j0:j0 + 1], AF.Exp,
                                 bias=Z128[:], scale=LNDc[:]), "SC")

    w("vector", "SC", sc_lnd)
    op("vector", lambda e: e.tensor_tensor(L0B[:], L0F[:], M0B[:], ALU.mult))
    dv_lprep = op("vector", lambda e: e.tensor_copy(WCb[:], WCf[:]), "DV")

    if phases <= 0.5:
        return finish_debug(a2, at)

    pa = ExitStack()
    CP = [pa.enter_context(nc.psum_tensor(f"cp{i}", [128, 512], F32)) for i in range(4)]

    w("tensor", "WCC", wt_ccin)
    pe_cc = [0] * 16
    dv_ccevac = [0] * 16
    stat_fin = [0] * TN
    agg_fin = [0] * TN
    last_chain_end = {0: 0, 1: 0}
    for g in range(16):
        dj, fc = g // 2, g % 2
        if g >= 4:
            w("tensor", "SC", dv_ccevac[g - 4])
        for k in range(ND):
            pe_cc[g] = op("tensor", lambda e, dj=dj, fc=fc, k=k, g=g:
                          e.matmul(CP[g % 4][:], CBTB[:, 1024 * k + 128 * dj:1024 * k + 128 * dj + 128],
                                   CUBB[:, 1024 * k + 512 * fc:1024 * k + 512 * fc + 512],
                                   start=(k == 0), stop=(k == ND - 1)),
                          "PE" if k == ND - 1 else None)
        w("scalar", "PE", pe_cc[g])
        dv_ccevac[g] = op("scalar", lambda e, dj=dj, fc=fc, g=g:
                          e.activation(HC[dj][:, 512 * fc:512 * fc + 512], CP[g % 4][:],
                                       AF.Identity, bias=0.0, scale=1.0), "SC")
        if phases <= 0.77:
            continue
        t = g
        if t % 4 == 0:
            w("vector", f"LDX{t // 4 + 1}".replace("LDX1", "LDX"), 64)
        p = 12 * (t % 2)
        if t >= 2:
            w("vector", "DV", agg_fin[t - 2])
        op("vector", lambda e, t=t, p=p: e.bn_stats(STA[:, p:p + 6], X[t][:, 0:512]))
        stat_fin[t] = op("vector", lambda e, t=t, p=p: e.bn_stats(STA[:, p + 6:p + 12], X[t][:, 512:1024]), "DV")
        if t >= 1:
            q = 12 * ((t - 1) % 2)
            w("vector", "DV", stat_fin[t - 1])
            op("vector", lambda e, t=t, q=q: e.bn_aggr(aggr_out(MV1, t - 1), STA[:, q:q + 6]))
            gap("vector", 64)
            agg_fin[t - 1] = op("vector", lambda e, t=t, q=q: e.bn_aggr(aggr_out(MV2, t - 1), STA[:, q + 6:q + 12]), "DV")
    if phases <= 0.8:
        return finish_debug(pa, a2, at)
    w("vector", "DV", stat_fin[15])
    q = 12 * (15 % 2)
    op("vector", lambda e, q=q: e.bn_aggr(aggr_out(MV1, 15), STA[:, q:q + 6]))
    gap("vector", 64)
    fin = op("vector", lambda e, q=q: e.bn_aggr(aggr_out(MV2, 15), STA[:, q + 6:q + 12]), "DV")
    w("vector", "DV", fin)

    def ln_combine(n0, n, Mdst, Rdst, NBdst, fold=1.0):
        ms, ds, vs_, d2, vh = WS[0], WS[1], WS[2], WS[3], WS[4]
        sl = lambda T: T[:, n0:n0 + n]
        op("vector", lambda e: e.tensor_tensor(sl(ms), mv_m(MV1, n0, n), mv_m(MV2, n0, n), ALU.add))
        op("vector", lambda e: e.tensor_tensor(sl(ds), mv_m(MV1, n0, n), mv_m(MV2, n0, n), ALU.subtract))
        v = op("vector", lambda e: e.tensor_tensor(sl(vs_), mv_v(MV1, n0, n), mv_v(MV2, n0, n), ALU.add), "DV")
        w("vector", "DV", v)
        op("vector", lambda e: e.tensor_tensor(sl(d2), sl(ds), sl(ds), ALU.mult))
        op("vector", lambda e: e.tensor_scalar(sl(vh), sl(vs_), 0.5, None, ALU.mult))
        v = op("vector", lambda e: e.tensor_scalar(sl(Mdst), sl(ms), 0.5, None, ALU.mult), "DV")
        w("vector", "DV", v)
        dv = op("vector", lambda e: e.scalar_tensor_tensor(sl(vs_), sl(d2), 0.25, sl(vh),
                                                           ALU.mult, ALU.add), "DV")
        w("scalar", "DV", dv)
        sc = op("scalar", lambda e: e.activation(sl(WS[5]), sl(vs_), AF.Sqrt,
                                                 bias=EPSc[:], scale=1.0), "SC")
        w("vector", "SC", sc)
        v = op("vector", lambda e: e.reciprocal(sl(Rdst), sl(WS[5])), "DV")
        w("vector", "DV", v)
        if fold != 1.0:
            v = op("vector", lambda e: e.tensor_scalar(sl(Rdst), sl(Rdst), fold, None, ALU.mult), "DV")
            w("vector", "DV", v)
        return op("vector", lambda e: e.scalar_tensor_tensor(sl(NBdst), sl(Mdst), -1.0, sl(Rdst),
                                                             ALU.mult, ALU.mult), "DV")

    if phases <= 0.85:
        return finish_debug(pa, a2, at)

    dv_ln1 = ln_combine(0, 16, M1c, R1c, NB1c)

    gap("vector")
    dv_hcg = 0
    w("vector", "LDC", ld_consts)
    w("vector", "SC", dv_ccevac[15])
    for j in range(ND):
        dv_hcg = op("vector", lambda e, j=j:
                    e.tensor_scalar(HC[j][:], HC[j][:], GATB[:], None, ALU.mult), "DV")

    if phases <= 0.9:
        return finish_debug(pa, a2, at)

    sc_x1 = [0] * TN

    def emit_x1_norm(t):
        w("vector", "DV", dv_ln1)
        sc_x1[t] = op("vector", lambda e, t=t:
                      e.tensor_scalar(XW[t][:], X[t][:], R1c[:, t:t + 1], NB1c[:, t:t + 1],
                                      ALU.mult, ALU.add), "DV")

    for t in range(4):
        emit_x1_norm(t)

    pa.close()
    pb = ExitStack()
    PB = [pb.enter_context(nc.psum_tensor(f"pb{i}", [128, 512], F32)) for i in range(8)]

    pe_scan = [[0] * 8 for _ in range(NCH)]
    pe_mv = [0] * NCH
    dv_ztevac = [[0] * 8 for _ in range(NCH)]
    dv_state = [0] * NCH

    if phases <= 1:
        return finish_debug(pa, a2, at)

    w("tensor", "DV", dv_lprep)
    w("tensor", "SC", dv_ccevac[15])

    a2.close()
    DUMP_TILES = (0, 15)
    STDMP = [at.enter_context(nc.sbuf_tensor(f"stdmp{t}", [128, 8], F32)) for t in range(2)] if phases < 90 else None
    ZT = [at.enter_context(nc.sbuf_tensor(f"zt{d}", [128, 2048], BF16)) for d in range(ND)]
    T1A = [at.enter_context(nc.sbuf_tensor(f"t1a{i}", [128, 1024], F32)) for i in range(2)]
    SQD1 = at.enter_context(nc.sbuf_tensor("sqd", [128, 1024], BF16))
    SQD = [SQD1, SQD1]

    w("scalar", "DV", dv_hcg)
    for I in range(NCH):
        w("tensor", "DV", sc_x1[4 * I + 3])
        if I > 0:
            w("tensor", "DV", dv_state[I - 1])
        for d in range(ND):
            if I > 0:
                w("tensor", "SC", dv_ztevac[I - 1][d])
                op("tensor", lambda e, d=d:
                   e.matmul(PB[d][:, 0:512], Srowb[0:1, 128 * d:128 * d + 128], DPb[0:1, :],
                            start=True, stop=False, skip_group_check=True))
            for sg in range(4):
                last = (sg == 3)
                pe_scan[I][d] = op("tensor", lambda e, I=I, d=d, sg=sg:
                                   e.matmul(PB[d][:, 128 * sg:512],
                                            XW[4 * I + sg][:, 128 * d:128 * d + 128],
                                            L0B[:, 0:512 - 128 * sg],
                                            start=(I == 0 and sg == 0), stop=(sg == 3),
                                            skip_group_check=True),
                                   "PE" if last else None)
        for d in range(2):
            w("scalar", "PE", pe_scan[I][d])
            dv_ztevac[I][d] = op("scalar", lambda e, I=I, d=d:
                                 e.activation(ZT[d][:, 512 * I:512 * I + 512], PB[d][:, 0:512],
                                              AF.Identity, bias=0.0, scale=1.0), "SC")
        if I < NCH - 1:
            w("tensor", "SC", dv_ztevac[I][1])
            for dc in range(2):
                for sg in range(4):
                    pe_mv[I] = op("tensor", lambda e, I=I, dc=dc, sg=sg:
                                  e.matmul(PB[dc][0:1, 0:512], WCb[:, sg:sg + 1],
                                           XW[4 * I + sg][:, 512 * dc:512 * dc + 512],
                                           start=(sg == 0), stop=(sg == 3),
                                           skip_group_check=True),
                                  "PE" if (sg == 3) else None)
        for d in range(2, ND):
            w("scalar", "PE", pe_scan[I][d])
            dv_ztevac[I][d] = op("scalar", lambda e, I=I, d=d:
                                 e.activation(ZT[d][:, 512 * I:512 * I + 512], PB[d][:, 0:512],
                                              AF.Identity, bias=0.0, scale=1.0), "SC")
        if I < NCH - 1:
            for t_ in range(4 * (I + 1), 4 * (I + 2)):
                emit_x1_norm(t_)
        if I < NCH - 1:
            w("vector", "PE", pe_mv[I])
            w("vector", "DV", dv_init if I == 0 else dv_state[I - 1])
            op("vector", lambda e: e.scalar_tensor_tensor(
                Srowb[0:1, 0:512], Srowb[0:1, 0:512], D512[0:1, 0:1],
                PB[0][0:1, 0:512], ALU.mult, ALU.add))
            dv_state[I] = op("vector", lambda e: e.scalar_tensor_tensor(
                Srowb[0:1, 512:1024], Srowb[0:1, 512:1024], D512[0:1, 0:1],
                PB[1][0:1, 0:512], ALU.mult, ALU.add), "DV")

    if phases <= 2:
        return finish_debug(pb, at)

    INVD = 1.0 / 1024.0

    def lnacc_s1(PSRC, Tb, c, wait_pe, t_free_dv):
        w("scalar", "PE", wait_pe)
        w("scalar", "DV", t_free_dv)
        return op("scalar", lambda e: e.activation(
            Tb[:], PSRC[:], AF.Identity, bias=0.0, scale=1.0,
            accum_out=SCR[:, c:c + 1]), "SC")

    def lnacc_s2(Tb, SQb, c, sc_cp):
        w("vector", "SC", sc_cp)
        op("vector", lambda e: e.tensor_scalar(SCR[:, c:c + 1], SCR[:, c:c + 1], INVD, None, ALU.mult))
        v = op("vector", lambda e: e.scalar_tensor_tensor(
            SQb[:], Tb[:], 1.0, Tb[:], ALU.mult, ALU.mult, accum_out=SCR[:, c + 1:c + 2]), "DV")
        w("vector", "DV", v)
        v = op("vector", lambda e: e.scalar_tensor_tensor(
            SCR[:, c + 2:c + 3], SCR[:, c:c + 1], 1.0, SCR[:, c:c + 1], ALU.mult, ALU.mult), "DV")
        w("vector", "DV", v)
        dv_v = op("vector", lambda e: e.tensor_scalar(
            SCR[:, c + 3:c + 4], SCR[:, c + 1:c + 2], INVD, SCR[:, c + 2:c + 3],
            ALU.mult, ALU.subtract), "DV")
        w("scalar", "DV", dv_v)
        sc_sq = op("scalar", lambda e: e.activation(
            SCR[:, c + 4:c + 5], SCR[:, c + 3:c + 4], AF.Sqrt, bias=EPSc[:], scale=1.0), "SC")
        return sc_sq

    def lnacc_s3(Tb, c, Xt, sc_sq, x_war_dv):
        w("vector", "SC", sc_sq)
        v = op("vector", lambda e: e.reciprocal(SCR[:, c + 4:c + 5], SCR[:, c + 4:c + 5]), "DV")
        w("vector", "DV", v)
        dv_nb = op("vector", lambda e: e.scalar_tensor_tensor(
            SCR[:, c + 5:c + 6], SCR[:, c:c + 1], -1.0, SCR[:, c + 4:c + 5], ALU.mult, ALU.mult), "DV")
        w("scalar", "DV", dv_nb)
        sc_nrm = op("scalar", lambda e: e.activation(
            Tb[:], Tb[:], AF.Identity, bias=SCR[:, c + 5:c + 6], scale=SCR[:, c + 4:c + 5]), "SC")
        w("vector", "SC", sc_nrm)
        w("vector", "DV", x_war_dv)
        dv_xadd = op("vector", lambda e: e.tensor_tensor(Xt[:], Xt[:], Tb[:], ALU.add), "DV")
        return sc_nrm, dv_xadd

    pb.close()
    pc = ExitStack()
    PR = [pc.enter_context(nc.psum_tensor(f"pr{i}", [128, 1024], F32)) for i in range(4)]

    pe_retr = [0] * TN
    sc_cp = [0] * TN
    sc_sqa = [0] * TN
    dv_nrm_a = [0] * TN
    dv_xattn = [0] * TN

    def attn_stage1(t):
        p, pr = t % 4, t % 2
        return lnacc_s1(PR[p], T1A[pr], 6 * pr,
                        wait_pe=pe_retr[t],
                        t_free_dv=dv_xattn[t - 2] if t >= 2 else 0)

    def attn_stage2(t):
        pr = t % 2
        return lnacc_s2(T1A[pr], SQD[pr], 6 * pr, sc_cp[t])

    def attn_stage3(t):
        pr = t % 2
        nrm, xadd = lnacc_s3(T1A[pr], 6 * pr, X[t], sc_sqa[t], 0)
        dv_nrm_a[t] = nrm
        dv_xattn[t] = xadd

    for t in range(TN):
        p = t % 4
        w("tensor", "SC", dv_ztevac[NCH - 1][7])
        w("tensor", "DV", dv_hcg)
        if t >= 4:
            w("tensor", "SC", sc_cp[t - 4])
        for fc in range(2):
            for d in range(ND):
                op("tensor", lambda e, t=t, p=p, fc=fc, d=d:
                   e.matmul(PR[p][:, 512 * fc:512 * fc + 512],
                            ZT[d][:, 128 * t:128 * t + 128],
                            HC[d][:, 512 * fc:512 * fc + 512],
                            start=(d == 0), stop=False, skip_group_check=True))
            pe_retr[t] = op("tensor", lambda e, t=t, p=p, fc=fc:
                            e.matmul(PR[p][:, 512 * fc:512 * fc + 512], IDB[:],
                                     XW[t][:, 512 * fc:512 * fc + 512],
                                     start=False, stop=True, skip_group_check=True), "PE")
        if t >= 2:
            attn_stage3(t - 2)
        if t >= 1:
            sc_sqa[t - 1] = attn_stage2(t - 1)
        sc_cp[t] = attn_stage1(t)
    sc_sqa[TN - 1] = attn_stage2(TN - 1)
    attn_stage3(TN - 2)
    attn_stage3(TN - 1)
    if phases < 90:
        for i, t in enumerate(DUMP_TILES):
            w("vector", "DV", dv_xattn[t])
            op("vector", lambda e, i=i, c=6 * (t % 2): e.tensor_copy(STDMP[i][:, 0:6], SCR[:, c:c + 6]))

    if phases <= 3:
        return finish_debug(pc, at, dump=[(zt_o, ZT), (hc_o, HC), (xw_o, XW), (st_o, STDMP)])

    ffps_pr0 = PR[0]
    pc.close()
    at.close()

    ff = ExitStack()
    fsb = lambda n, shp, dt: ff.enter_context(nc.sbuf_tensor(n, shp, dt))
    X2T = fsb("x2t", [128, 8192], BF16)
    X2S = [fsb(f"x2s{i}", [128, 1024], BF16) for i in range(2)]
    SIL = [fsb(f"sil{i}", [128, 1024], BF16) for i in range(2)]
    T1FA = [fsb(f"t1f{i}", [128, 1024], F32) for i in range(2)]
    SQDF1 = fsb("sqdf", [128, 1024], BF16)
    SQDF = [SQDF1, SQDF1]
    WGU = [fsb(f"wgu{i}", [128, 2048], BF16) for i in range(2)]

    ffps = {"PR0": ffps_pr0}
    prev_dn_sc = 0
    slot_pe = [0, 0]
    prev_half_sc = sc_cp[15]
    prev_half_dv = dv_xattn[15]

    w("gpsimd", "PE", pe_retr[15])
    w("gpsimd", "DV", dv_xattn[15])
    w("gpsimd", "SC", sc_cp[15])
    w("scalar", "PE", pe_retr[15])
    w("scalar", "DV", dv_xattn[15])
    w("vector", "PE", pe_retr[15])
    w("vector", "SC", sc_cp[15])
    dv_xfin = [0] * TN

    def emit_ln2(hf, war_dv):
        tg0 = 8 * hf
        w("vector", "DV", war_dv)
        for tl in range(8):
            t = tg0 + tl
            p = 12 * (tl % 2)
            w("vector", "DV", dv_xattn[t])
            if tl >= 2:
                w("vector", "DV", agg_fin[tl - 2])
            op("vector", lambda e, t=t, p=p: e.bn_stats(STA[:, p:p + 6], X[t][:, 0:512]))
            stat_fin[tl] = op("vector", lambda e, t=t, p=p: e.bn_stats(STA[:, p + 6:p + 12], X[t][:, 512:1024]), "DV")
            if tl >= 1:
                q = 12 * ((tl - 1) % 2)
                w("vector", "DV", stat_fin[tl - 1])
                op("vector", lambda e, t=t - 1, q=q: e.bn_aggr(aggr_out(MV1, t), STA[:, q:q + 6]))
                gap("vector", 64)
                agg_fin[tl - 1] = op("vector", lambda e, t=t - 1, q=q: e.bn_aggr(aggr_out(MV2, t), STA[:, q + 6:q + 12]), "DV")
        w("vector", "DV", stat_fin[7])
        q = 12 * (7 % 2)
        op("vector", lambda e, t=tg0 + 7, q=q: e.bn_aggr(aggr_out(MV1, t), STA[:, q:q + 6]))
        gap("vector", 64)
        fin = op("vector", lambda e, t=tg0 + 7, q=q: e.bn_aggr(aggr_out(MV2, t), STA[:, q + 6:q + 12]), "DV")
        w("vector", "DV", fin)
        return ln_combine(tg0, 8, M1c, R2s, NB2s, fold=INV127)

    def emit_x2pipe(hf, dv_ln2, psum_sc, psum_dv, prev_pe_tr):
        tg0 = 8 * hf
        PT = ffps["PT"]
        sc_x2 = [0] * 8
        pe_tr = [0] * 8
        dv_pt = [0] * 8
        for tl in range(8):
            t = tg0 + tl
            w("scalar", "DV", dv_ln2)
            w("scalar", "PE", pe_tr[tl - 2] if tl >= 2 else prev_pe_tr[tl - 2])
            sc_x2[tl] = op("scalar", lambda e, t=t, tl=tl: e.activation(
                X2S[tl % 2][:], X[t][:], AF.Identity,
                bias=NB2s[:, t:t + 1], scale=R2s[:, t:t + 1]), "SC")
            w("tensor", "SC", sc_x2[tl])
            if tl == 0:
                w("tensor", "SC", psum_sc)
                w("tensor", "DV", psum_dv)
            else:
                w("tensor", "DV", dv_pt[tl - 1])
            for d in range(ND):
                pe_tr[tl] = op("tensor", lambda e, tl=tl, d=d: e.transpose(
                    PT[:, 128 * d:128 * d + 128], X2S[tl % 2][:, 128 * d:128 * d + 128], IDB[:]),
                    "PE" if d == ND - 1 else None)
            w("vector", "PE", pe_tr[tl])
            dv_pt[tl] = op("vector", lambda e, tl=tl: e.tensor_copy(
                X2T[...].rearrange("p (d t) -> p d t", d=ND)[:, :, 128 * tl:128 * tl + 128],
                PT[...].rearrange("p (d t) -> p d t", d=ND)[:, :, :]), "DV")
        return sc_x2, pe_tr, dv_pt

    pt_scope = ExitStack()
    ffps["PT"] = pt_scope.enter_context(nc.psum_tensor("pt", [128, 1024], BF16))
    pt_scope.close()

    dv_ln2_done = 0
    x2pipe_B = None

    for hf in range(2):
        tg0 = 8 * hf

        if hf == 0:
            dv_ln2_done = emit_ln2(0, 0)
            for wv in (stat_fin[1], stat_fin[3], stat_fin[5], stat_fin[7],
                       agg_fin[6], dv_ln2_done):
                w("tensor", "DV", wv)
                op("tensor", lambda e: e.matmul(
                    ffps["PR0"][0:64, 0:64], XW[0][:, 0:64], XW[0][:, 0:64],
                    start=True, stop=True, skip_group_check=True))
            sc_x2, pe_tr, dv_pt = emit_x2pipe(0, dv_ln2_done, prev_half_sc, prev_half_dv,
                                              [pe_retr[15]] * 2)
        else:
            sc_x2, pe_tr, dv_pt = x2pipe_B
        if phases <= 4:
            return finish_debug(ff)

        if hf == 0:
            pg_scope = ExitStack()
            PG = [pg_scope.enter_context(nc.psum_tensor(f"pg{i}", [128, 1024], F32)) for i in range(2)]
            PU = [pg_scope.enter_context(nc.psum_tensor(f"pu{i}", [128, 1024], F32)) for i in range(2)]
            ffps["PG"], ffps["PU"] = PG, PU
        else:
            PG, PU = ffps["PG"], ffps["PU"]
        wt_gu = [0] * NH
        pe_g = [[0] * 2 for _ in range(NH)]
        pe_u = [[0] * 2 for _ in range(NH)]
        sc_sil = [[0] * 2 for _ in range(NH)]
        dv_ht = [[0] * 2 for _ in range(NH)]

        def HT(eta):
            return XW[eta] if eta < 16 else HC[eta - 16]

        for eta in range(NH):
            sl = eta % 2
            if hf == 1 and eta == 0:
                w("tensor", "SC", prev_dn_sc)
            w("gpsimd", "PE", pe_u[eta - 2][1] if eta >= 2 else slot_pe[sl])
            wt_gu[eta] = dma("gpsimd", WGU[sl][:], wgu[eta, :, :], f"WG{sl}")
            w("tensor", f"WG{sl}", wt_gu[eta])
            w("tensor", "DV", dv_pt[7])
            if eta >= 2:
                w("tensor", "SC", sc_sil[eta - 2][1])
                w("tensor", "DV", dv_ht[eta - 2][1])
            for c in range(2):
                for d in range(ND):
                    pe_g[eta][c] = op("tensor", lambda e, sl=sl, c=c, d=d: e.matmul(
                        PG[sl][:, 512 * c:512 * c + 512],
                        WGU[sl][:, 128 * d:128 * d + 128],
                        X2T[:, 1024 * d + 512 * c:1024 * d + 512 * c + 512],
                        start=(d == 0), stop=(d == ND - 1)),
                        "PE" if d == ND - 1 else None)
            for c in range(2):
                for d in range(ND):
                    pe_u[eta][c] = op("tensor", lambda e, sl=sl, c=c, d=d: e.matmul(
                        PU[sl][:, 512 * c:512 * c + 512],
                        WGU[sl][:, 1024 + 128 * d:1024 + 128 * d + 128],
                        X2T[:, 1024 * d + 512 * c:1024 * d + 512 * c + 512],
                        start=(d == 0), stop=(d == ND - 1)),
                        "PE" if d == ND - 1 else None)
            slot_pe[sl] = pe_u[eta][1]
            if hf == 0 and eta == 6:
                dv_ln2_done = emit_ln2(1, dv_ln2_done)
            for c in range(2):
                w("scalar", "PE", pe_g[eta][c])
                if eta >= 2:
                    w("scalar", "DV", dv_ht[eta - 2][c])
                sc_sil[eta][c] = op("scalar", lambda e, sl=sl, c=c: e.activation(
                    SIL[sl][:, 512 * c:512 * c + 512], PG[sl][:, 512 * c:512 * c + 512],
                    AF.Silu, bias=Z128[:], scale=1.0), "SC")
            for c in range(2):
                w("vector", "SC", sc_sil[eta][c])
                w("vector", "PE", pe_u[eta][c])
                dv_ht[eta][c] = op("vector", lambda e, eta=eta, sl=sl, c=c: e.scalar_tensor_tensor(
                    HT(eta)[:, 512 * c:512 * c + 512], SIL[sl][:, 512 * c:512 * c + 512],
                    INV127, PU[sl][:, 512 * c:512 * c + 512], ALU.mult, ALU.mult), "DV")
        if hf == 0:
            pg_scope.close()
        if phases <= 5:
            return finish_debug(ff)

        if hf == 0:
            pf_scope = ExitStack()
            PF = [pf_scope.enter_context(nc.psum_tensor(f"pf{i}", [128, 1024], F32)) for i in range(2)]
            ffps["PF"] = PF
        else:
            PF = ffps["PF"]
        PFS = [PF[0], PF[1], PU[0], PU[1]]
        pe_dn = [0] * 8
        sc_cpf = [0] * 8
        sc_sqf = [0] * 8
        dv_nrmf = [0] * 8

        def dn_stage1(tl):
            pr = tl % 2
            return lnacc_s1(PFS[tl % 4], T1FA[pr], 6 * pr,
                            wait_pe=pe_dn[tl],
                            t_free_dv=dv_xfin[tg0 + tl - 2] if tl >= 2 else
                            (dv_xfin[7] if hf == 1 else 0))

        def dn_stage2(tl):
            pr = tl % 2
            return lnacc_s2(T1FA[pr], SQDF[pr], 6 * pr, sc_cpf[tl])

        def dn_stage3(tl):
            t = tg0 + tl
            pr = tl % 2
            nrm, xadd = lnacc_s3(T1FA[pr], 6 * pr, X[t], sc_sqf[tl], dv_xattn[t])
            dv_nrmf[tl] = nrm
            dv_xfin[t] = xadd
            w("sync", "DV", dv_xfin[t])
            dma("sync", out[t, :, :], X[t][:], "ST")

        w("tensor", "WWD", wt_wd)
        w("tensor", "DV", dv_ht[NH - 1][1])
        w("tensor", "SC", sc_sil[NH - 1][1])
        for tl in range(8):
            t = tg0 + tl
            sl = tl % 4
            if tl >= 4:
                w("tensor", "SC", sc_cpf[tl - 4])
            for fc in range(2):
                for eta in range(NH):
                    pe_dn[tl] = op("tensor", lambda e, sl=sl, tl=tl, fc=fc, eta=eta: e.matmul(
                        PFS[sl][:, 512 * fc:512 * fc + 512],
                        HT(eta)[:, 128 * tl:128 * tl + 128],
                        WDTB[:, 1024 * eta + 512 * fc:1024 * eta + 512 * fc + 512],
                        start=(eta == 0), stop=(eta == NH - 1)),
                        "PE" if (eta == NH - 1 and fc == 1) else None)
            if tl >= 2:
                dn_stage3(tl - 2)
            if tl >= 1:
                sc_sqf[tl - 1] = dn_stage2(tl - 1)
            sc_cpf[tl] = dn_stage1(tl)
            if hf == 0 and tl == 5:
                x2pipe_B = emit_x2pipe(1, dv_ln2_done, sc_cpf[4], 0, pe_tr)
        sc_sqf[7] = dn_stage2(7)
        dn_stage3(6)
        dn_stage3(7)
        if hf == 0:
            pf_scope.close()
        if phases <= 6 and hf == 0:
            return finish_debug(ff)
        prev_dn_sc = sc_cpf[7]

    ff.close()

    w("sync", "ST", C["ST"])

    with nc.Block() as block:
        @block.sync
        def _(e):
            for f in plan["sync"]:
                f(e)

        @block.gpsimd
        def _(e):
            for f in plan["gpsimd"]:
                f(e)

        @block.vector
        def _(e):
            for f in plan["vector"]:
                f(e)

        @block.scalar
        def _(e):
            for f in plan["scalar"]:
                f(e)

        @block.tensor
        def _(e):
            for f in plan["tensor"]:
                f(e)

    es.close()
    return nc


def kernel(L0_attn_x, L0_attn_bind_key, L0_attn_unbind_key, L0_attn_gate, L0_attn_decay,
           L0_ffn_w_gate, L0_ffn_w_up, L0_ffn_w_down):
    x = np.ascontiguousarray(np.asarray(L0_attn_x), dtype=np.float32)
    b = np.asarray(L0_attn_bind_key)
    u = np.asarray(L0_attn_unbind_key)
    gate = np.asarray(L0_attn_gate).reshape(1)
    decay = np.asarray(L0_attn_decay).reshape(())
    wg = np.asarray(L0_ffn_w_gate)
    wu = np.asarray(L0_ffn_w_up)
    wd = np.asarray(L0_ffn_w_down)

    D = 1024
    idx = np.arange(D)
    circ = (idx[:, None] - idx[None, :]) % D
    cbt = np.ascontiguousarray(b[circ].reshape(ND, 128, D).transpose(1, 0, 2).reshape(128, ND * D))
    cuT = np.ascontiguousarray(u[circ].reshape(ND, 128, D).transpose(1, 0, 2).reshape(128, ND * D))

    ii = np.arange(128)[:, None]
    jj = np.arange(512)[None, :]
    e0m = np.maximum(jj - ii, 0).astype(np.float32)
    cstf = np.ascontiguousarray(np.concatenate(
        [e0m, np.broadcast_to(decay, (128, 1)).astype(np.float32),
         np.broadcast_to(gate, (128, 1)).astype(np.float32)], axis=1))
    cstb = np.ascontiguousarray(np.concatenate(
        [(jj >= ii).astype(ml_dtypes.bfloat16), np.eye(128, dtype=ml_dtypes.bfloat16)], axis=1))

    wgt = wg.reshape(NH, 128, ND, 128).transpose(0, 3, 2, 1).reshape(NH, 128, 1024)
    wut = wu.reshape(NH, 128, ND, 128).transpose(0, 3, 2, 1).reshape(NH, 128, 1024)
    wgu_h = np.ascontiguousarray(np.concatenate([wgt, wut], axis=2))
    wdt = np.ascontiguousarray(wd.T.reshape(NH, 128, 1024).transpose(1, 0, 2).reshape(128, NH * 1024))

    nc = build_nc()
    in_maps = []
    for c in range(8):
        in_maps.append(dict(
            xin=np.ascontiguousarray(x[c].reshape(TN, 128, 1024).transpose(1, 0, 2).reshape(128, TN * 1024)),
            cbt=cbt, cuT=cuT, cstf=cstf, cstb=cstb,
            wgu=wgu_h, wdt=wdt,
        ))
    res = run_bass_kernel_spmd(nc, in_maps, core_ids=list(range(8)))
    return np.stack([res.results[c]["out"].reshape(2048, 1024) for c in range(8)], axis=0)


# version 20
# speedup vs baseline: 1.0400x; 1.0150x over previous
import numpy as np
import ml_dtypes
from contextlib import ExitStack

import concourse.bass as bass
import concourse.mybir as mybir
from concourse.bass_utils import run_bass_kernel_spmd

F32 = mybir.dt.float32
BF16 = mybir.dt.bfloat16
I32 = mybir.dt.int32
AF = mybir.ActivationFunctionType
ALU = mybir.AluOpType

TN = 16
ND = 8
NH = 24
NCH = 4
INV127 = 1.0 / 127.0


def build_nc(phases=99):
    nc = bass.Bass("TRN2", target_bir_lowering=False)

    xin = nc.dram_tensor("xin", [128, TN * 1024], F32, kind="ExternalInput")
    cbt = nc.dram_tensor("cbt", [128, ND * 1024], F32, kind="ExternalInput")
    cuT = nc.dram_tensor("cuT", [128, ND * 1024], F32, kind="ExternalInput")
    cstf = nc.dram_tensor("cstf", [128, 514], F32, kind="ExternalInput")
    cstb = nc.dram_tensor("cstb", [128, 640], BF16, kind="ExternalInput")
    wgu = nc.dram_tensor("wgu", [NH, 128, 2048], I32, kind="ExternalInput")
    wdt = nc.dram_tensor("wdt", [128, NH * 1024], I32, kind="ExternalInput")
    out = nc.dram_tensor("out", [TN, 128, 1024], F32, kind="ExternalOutput")
    if phases < 90:
        zt_o = nc.dram_tensor("zt_o", [ND, 128, 2048], BF16, kind="ExternalOutput")
        hc_o = nc.dram_tensor("hc_o", [ND, 128, 1024], BF16, kind="ExternalOutput")
        xw_o = nc.dram_tensor("xw_o", [TN, 128, 1024], BF16, kind="ExternalOutput")
        st_o = nc.dram_tensor("st_o", [2, 128, 8], F32, kind="ExternalOutput")

    plan = {k: [] for k in ("sync", "gpsimd", "vector", "scalar", "tensor")}
    C = {"LDX": 0, "LDX2": 0, "LDX3": 0, "LDX4": 0, "LDC": 0, "WCC": 0, "WWD": 0,
         "WG0": 0, "WG1": 0, "PE": 0, "DV": 0, "SC": 0, "GP": 0, "ST": 0}
    S = {}

    def w(eng, sem, val):
        if val and val > 0:
            plan[eng].append(lambda e, s=sem, v=val: e.wait_ge(S[s], v))

    _gapn = {"vector": 0, "scalar": 0}

    def gap(eng, cyc=300):
        n = max(1, cyc // 200)
        for _ in range(n):
            i = _gapn[eng] % 3
            _gapn[eng] += 1
            if eng == "vector":
                plan[eng].append(lambda e, i=i: e.tensor_copy(GAPT[:, i:i + 1], Z128[:]))
            else:
                plan[eng].append(lambda e, i=i: e.activation(GAPS[:, i:i + 1], Z128[:],
                                                             AF.Identity, bias=0.0, scale=1.0))

    def dma(eng, out_ap, in_ap, sem):
        C[sem] += 16
        plan[eng].append(lambda e, o=out_ap, i=in_ap, s=sem:
                         e.dma_start(out=o, in_=i).then_inc(S[s], 16))
        return C[sem]

    def op(eng, fn, sem=None):
        if sem is None:
            plan[eng].append(fn)
            return None
        C[sem] += 1
        plan[eng].append(lambda e, f=fn, s=sem: f(e).then_inc(S[s], 1))
        return C[sem]

    es = ExitStack()
    for name in C:
        S[name] = es.enter_context(nc.semaphore(name))

    sb = lambda n, shp, dt: es.enter_context(nc.sbuf_tensor(n, shp, dt))

    X = [sb(f"x{t}", [128, 1024], F32) for t in range(TN)]
    XW = [sb(f"xw{t}", [128, 1024], BF16) for t in range(TN)]
    HC = [sb(f"hc{j}", [128, 1024], BF16) for j in range(ND)]
    WDTB = sb("wdtb", [128, NH * 1024], BF16)
    IDBT = sb("idb_s", [128, 128], BF16)
    M1c = sb("m1c", [128, 16], F32)
    R1c = sb("r1c", [128, 16], F32)
    NB1c = sb("nb1c", [128, 16], F32)
    R2s = sb("r2s", [128, 16], F32)
    NB2s = sb("nb2s", [128, 16], F32)
    STA = sb("sta", [128, 24], F32)
    MV1 = sb("mv1", [128, 32], F32)
    MV2 = sb("mv2", [128, 32], F32)
    WS = [sb(f"ws{i}", [128, 16], F32) for i in range(6)]
    SCR = sb("scr", [128, 12], F32)
    LMV = sb("lmv", [128, 8], F32)
    LSTA = sb("lsta", [128, 24], F32)
    EPSc = sb("epsc", [128, 1], F32)
    Z128 = sb("z128", [128, 1], F32)
    SIGc = sb("sigc", [128, 1], F32)
    LNDc = sb("lndc", [128, 1], F32)
    GAPT = sb("gapt", [128, 3], F32)
    GAPS = sb("gaps", [128, 3], F32)

    def mv_m(MV, lo, n):
        return MV[:, 0 + lo: 0 + lo + n]

    def mv_v(MV, lo, n):
        return MV[:, 16 + lo: 16 + lo + n]

    def aggr_out(MV, t):
        return MV[:, :].rearrange("p (two t) -> p two t", two=2)[:, :, t:t + 1]

    at = ExitStack()
    asb = lambda n, shp, dt: at.enter_context(nc.sbuf_tensor(n, shp, dt))
    L0B = asb("l0b", [128, 512], BF16)
    DPb = asb("dpb", [1, 512], BF16)
    D512 = asb("d512", [1, 1], F32)
    WCf = asb("wcf", [128, 4], F32)
    WCb = asb("wcb", [128, 4], BF16)
    Srowb = asb("srowb", [1, 1024], BF16)

    a2 = ExitStack()
    a2sb = lambda n, shp, dt: a2.enter_context(nc.sbuf_tensor(n, shp, dt))
    CBTB = a2sb("cbtb", [128, ND * 1024], BF16)
    CUBB = a2sb("cubb", [128, ND * 1024], BF16)
    L0F = a2sb("l0f", [128, 512], BF16)
    CSTF = a2sb("cstf_s", [128, 514], F32)
    CSTB = a2sb("cstb_s", [128, 512], BF16)

    class _CV:
        def __init__(self, T, f0, fN):
            self.T, self.f0, self.fN = T, f0, fN

        def __getitem__(self, key):
            if key == slice(None):
                return self.T[:, self.f0:self.f0 + self.fN]
            p, f = key
            fs = self.f0 + (f.start or 0)
            fe = self.f0 + (f.stop if f.stop is not None else self.fN)
            return self.T[p, fs:fe]

    E0M = _CV(CSTF, 0, 512)
    DECB = _CV(CSTF, 512, 1)
    GATB = _CV(CSTF, 513, 1)
    M0B = _CV(CSTB, 0, 512)
    IDB = IDBT

    dma("gpsimd", CSTF[:], cstf[:, :], "LDC")
    dma("gpsimd", CSTB[:], cstb[:, 0:512], "LDC")
    ld_consts = dma("gpsimd", IDBT[:], cstb[:, 512:640], "LDC")
    ldx_tot = [4 * 16] * 4

    for q in range(4):
        gsem = f"LDX{q + 1}".replace("LDX1", "LDX")
        dma("gpsimd", CBTB[:, 2048 * q:2048 * q + 2048], cbt[:, 2048 * q:2048 * q + 2048], gsem)
        dma("gpsimd", CUBB[:, 2048 * q:2048 * q + 2048], cuT[:, 2048 * q:2048 * q + 2048], gsem)
        for t in range(4 * q, 4 * q + 4):
            dma("gpsimd", X[t][:], xin[:, 1024 * t:1024 * t + 1024], gsem)
    wt_wd = dma("gpsimd", WDTB[:], wdt[:, :], "WWD")

    op("vector", lambda e: e.memset(GAPT[:], 0.0))
    op("vector", lambda e: e.memset(Z128[:], 0.0))
    op("vector", lambda e: e.memset(EPSc[:], 1e-5))
    dv_init = op("vector", lambda e: e.memset(Srowb[:], 0.0), "DV")

    def finish_debug(*inner_stacks, dump=()):
        w("sync", "DV", C["DV"])
        w("sync", "SC", C["SC"])
        w("sync", "PE", C["PE"])
        for snm in ("WCC", "WWD", "WG0", "WG1"):
            w("sync", snm, C[snm])
        for t in range(TN):
            dma("sync", out[t, :, :], X[t][:], "ST")
        for dst, tiles in dump:
            for i, tile in enumerate(tiles):
                dma("sync", dst[i, :, :], tile[:], "ST")
        w("sync", "ST", C["ST"])
        with nc.Block() as block:
            block.sync(lambda e: [f(e) for f in plan["sync"]] and None)
            block.gpsimd(lambda e: [f(e) for f in plan["gpsimd"]] and None)
            block.vector(lambda e: [f(e) for f in plan["vector"]] and None)
            block.scalar(lambda e: [f(e) for f in plan["scalar"]] and None)
            block.tensor(lambda e: [f(e) for f in plan["tensor"]] and None)
        for st in inner_stacks:
            st.close()
        es.close()
        return nc

    if phases <= 0.25:
        return finish_debug(a2, at)

    w("scalar", "LDC", ld_consts)
    w("scalar", "DV", dv_init)
    sc_sig = op("scalar", lambda e: e.activation(SIGc[:], DECB[:], AF.Sigmoid,
                                                 bias=Z128[:], scale=1.0), "SC")
    w("vector", "SC", sc_sig)
    dv_sig = op("vector", lambda e: e.tensor_scalar(SIGc[:], SIGc[:], 1e-38, None, ALU.max), "DV")
    w("scalar", "DV", dv_sig)
    scv = op("scalar", lambda e: e.activation(LNDc[:], SIGc[:], AF.Ln, bias=Z128[:], scale=1.0), "SC")
    w("scalar", "SC", scv)
    op("scalar", lambda e: e.activation(L0F[:], E0M[:], AF.Exp, bias=Z128[:], scale=LNDc[:]))
    op("scalar", lambda e: e.activation(DPb[:], E0M[0:1, :], AF.Exp,
                                        bias=LNDc[0:1, :], scale=LNDc[0:1, :]))
    op("scalar", lambda e: e.activation(D512[:], E0M[0:1, 511:512], AF.Exp,
                                        bias=LNDc[0:1, :], scale=LNDc[0:1, :]))
    sc_lnd = 0
    for sg in range(4):
        j0 = 511 - 128 * sg
        sc_lnd = op("scalar", lambda e, sg=sg, j0=j0:
                    e.activation(WCf[:, sg:sg + 1], E0M[:, j0:j0 + 1], AF.Exp,
                                 bias=Z128[:], scale=LNDc[:]), "SC")

    w("vector", "SC", sc_lnd)
    op("vector", lambda e: e.tensor_tensor(L0B[:], L0F[:], M0B[:], ALU.mult))
    dv_lprep = op("vector", lambda e: e.tensor_copy(WCb[:], WCf[:]), "DV")

    if phases <= 0.5:
        return finish_debug(a2, at)

    pa = ExitStack()
    CP = [pa.enter_context(nc.psum_tensor(f"cp{i}", [128, 512], F32)) for i in range(8)]

    pe_cc = [0] * 16
    dv_ccevac = [0] * 16
    stat_fin = [0] * TN
    agg_fin = [0] * TN
    last_chain_end = {0: 0, 1: 0}
    for half in range(2):
        g0 = 8 * half
        if half == 1:
            w("tensor", "SC", dv_ccevac[7])
        for k in range(ND):
            if half == 0:
                w("tensor", f"LDX{k // 2 + 1}".replace("LDX1", "LDX"), 96)
            for g in range(g0, g0 + 8):
                dj, fc = g // 2, g % 2
                pe_cc[g] = op("tensor", lambda e, dj=dj, fc=fc, k=k, g=g:
                              e.matmul(CP[g % 8][:],
                                       CBTB[:, 1024 * k + 128 * dj:1024 * k + 128 * dj + 128],
                                       CUBB[:, 1024 * k + 512 * fc:1024 * k + 512 * fc + 512],
                                       start=(k == 0), stop=(k == ND - 1)),
                              "PE" if k == ND - 1 else None)
        for g in range(g0, g0 + 8):
            dj, fc = g // 2, g % 2
            w("scalar", "PE", pe_cc[g])
            dv_ccevac[g] = op("scalar", lambda e, dj=dj, fc=fc, g=g:
                              e.activation(HC[dj][:, 512 * fc:512 * fc + 512], CP[g % 8][:],
                                           AF.Identity, bias=0.0, scale=1.0), "SC")
    for t in range(TN):
        if phases <= 0.77:
            continue
        if t % 4 == 0:
            w("vector", f"LDX{t // 4 + 1}".replace("LDX1", "LDX"), 96)
        p = 12 * (t % 2)
        if t >= 2:
            w("vector", "DV", agg_fin[t - 2])
        op("vector", lambda e, t=t, p=p: e.bn_stats(STA[:, p:p + 6], X[t][:, 0:512]))
        stat_fin[t] = op("vector", lambda e, t=t, p=p: e.bn_stats(STA[:, p + 6:p + 12], X[t][:, 512:1024]), "DV")
        if t >= 1:
            q = 12 * ((t - 1) % 2)
            w("vector", "DV", stat_fin[t - 1])
            op("vector", lambda e, t=t, q=q: e.bn_aggr(aggr_out(MV1, t - 1), STA[:, q:q + 6]))
            gap("vector", 64)
            agg_fin[t - 1] = op("vector", lambda e, t=t, q=q: e.bn_aggr(aggr_out(MV2, t - 1), STA[:, q + 6:q + 12]), "DV")
    if phases <= 0.8:
        return finish_debug(pa, a2, at)
    w("vector", "DV", stat_fin[15])
    q = 12 * (15 % 2)
    op("vector", lambda e, q=q: e.bn_aggr(aggr_out(MV1, 15), STA[:, q:q + 6]))
    gap("vector", 64)
    fin = op("vector", lambda e, q=q: e.bn_aggr(aggr_out(MV2, 15), STA[:, q + 6:q + 12]), "DV")
    w("vector", "DV", fin)

    def ln_combine(n0, n, Mdst, Rdst, NBdst, fold=1.0):
        ms, ds, vs_, d2, vh = WS[0], WS[1], WS[2], WS[3], WS[4]
        sl = lambda T: T[:, n0:n0 + n]
        op("vector", lambda e: e.tensor_tensor(sl(ms), mv_m(MV1, n0, n), mv_m(MV2, n0, n), ALU.add))
        op("vector", lambda e: e.tensor_tensor(sl(ds), mv_m(MV1, n0, n), mv_m(MV2, n0, n), ALU.subtract))
        v = op("vector", lambda e: e.tensor_tensor(sl(vs_), mv_v(MV1, n0, n), mv_v(MV2, n0, n), ALU.add), "DV")
        w("vector", "DV", v)
        op("vector", lambda e: e.tensor_tensor(sl(d2), sl(ds), sl(ds), ALU.mult))
        op("vector", lambda e: e.tensor_scalar(sl(vh), sl(vs_), 0.5, None, ALU.mult))
        v = op("vector", lambda e: e.tensor_scalar(sl(Mdst), sl(ms), 0.5, None, ALU.mult), "DV")
        w("vector", "DV", v)
        dv = op("vector", lambda e: e.scalar_tensor_tensor(sl(vs_), sl(d2), 0.25, sl(vh),
                                                           ALU.mult, ALU.add), "DV")
        w("scalar", "DV", dv)
        sc = op("scalar", lambda e: e.activation(sl(WS[5]), sl(vs_), AF.Sqrt,
                                                 bias=EPSc[:], scale=1.0), "SC")
        w("vector", "SC", sc)
        v = op("vector", lambda e: e.reciprocal(sl(Rdst), sl(WS[5])), "DV")
        w("vector", "DV", v)
        if fold != 1.0:
            v = op("vector", lambda e: e.tensor_scalar(sl(Rdst), sl(Rdst), fold, None, ALU.mult), "DV")
            w("vector", "DV", v)
        return op("vector", lambda e: e.scalar_tensor_tensor(sl(NBdst), sl(Mdst), -1.0, sl(Rdst),
                                                             ALU.mult, ALU.mult), "DV")

    if phases <= 0.85:
        return finish_debug(pa, a2, at)

    dv_ln1 = ln_combine(0, 16, M1c, R1c, NB1c)

    gap("vector")
    dv_hcg = 0
    w("vector", "LDC", ld_consts)
    w("vector", "SC", dv_ccevac[15])
    for j in range(ND):
        dv_hcg = op("vector", lambda e, j=j:
                    e.tensor_scalar(HC[j][:], HC[j][:], GATB[:], None, ALU.mult), "DV")

    if phases <= 0.9:
        return finish_debug(pa, a2, at)

    sc_x1 = [0] * TN

    def emit_x1_norm(t):
        w("vector", "DV", dv_ln1)
        sc_x1[t] = op("vector", lambda e, t=t:
                      e.tensor_scalar(XW[t][:], X[t][:], R1c[:, t:t + 1], NB1c[:, t:t + 1],
                                      ALU.mult, ALU.add), "DV")

    for t in range(4):
        emit_x1_norm(t)

    pa.close()
    pb = ExitStack()
    PB = [pb.enter_context(nc.psum_tensor(f"pb{i}", [128, 512], F32)) for i in range(8)]

    pe_scan = [[0] * 8 for _ in range(NCH)]
    pe_mv = [0] * NCH
    dv_ztevac = [[0] * 8 for _ in range(NCH)]
    dv_state = [0] * NCH

    if phases <= 1:
        return finish_debug(pa, a2, at)

    w("tensor", "DV", dv_lprep)
    w("tensor", "SC", dv_ccevac[15])

    a2.close()
    DUMP_TILES = (0, 15)
    STDMP = [at.enter_context(nc.sbuf_tensor(f"stdmp{t}", [128, 8], F32)) for t in range(2)] if phases < 90 else None
    ZT = [at.enter_context(nc.sbuf_tensor(f"zt{d}", [128, 2048], BF16)) for d in range(ND)]
    T1A = [at.enter_context(nc.sbuf_tensor(f"t1a{i}", [128, 1024], F32)) for i in range(2)]
    SQD1 = at.enter_context(nc.sbuf_tensor("sqd", [128, 1024], BF16))
    SQD = [SQD1, SQD1]

    w("scalar", "DV", dv_hcg)
    for I in range(NCH):
        w("tensor", "DV", sc_x1[4 * I + 3])
        if I > 0:
            w("tensor", "DV", dv_state[I - 1])
        for d in range(ND):
            if I > 0:
                w("tensor", "SC", dv_ztevac[I - 1][d])
                op("tensor", lambda e, d=d:
                   e.matmul(PB[d][:, 0:512], Srowb[0:1, 128 * d:128 * d + 128], DPb[0:1, :],
                            start=True, stop=False, skip_group_check=True))
            for sg in range(4):
                last = (sg == 3)
                pe_scan[I][d] = op("tensor", lambda e, I=I, d=d, sg=sg:
                                   e.matmul(PB[d][:, 128 * sg:512],
                                            XW[4 * I + sg][:, 128 * d:128 * d + 128],
                                            L0B[:, 0:512 - 128 * sg],
                                            start=(I == 0 and sg == 0), stop=(sg == 3),
                                            skip_group_check=True),
                                   "PE" if last else None)
        for d in range(2):
            w("scalar", "PE", pe_scan[I][d])
            dv_ztevac[I][d] = op("scalar", lambda e, I=I, d=d:
                                 e.activation(ZT[d][:, 512 * I:512 * I + 512], PB[d][:, 0:512],
                                              AF.Identity, bias=0.0, scale=1.0), "SC")
        if I < NCH - 1:
            w("tensor", "SC", dv_ztevac[I][1])
            for dc in range(2):
                for sg in range(4):
                    pe_mv[I] = op("tensor", lambda e, I=I, dc=dc, sg=sg:
                                  e.matmul(PB[dc][0:1, 0:512], WCb[:, sg:sg + 1],
                                           XW[4 * I + sg][:, 512 * dc:512 * dc + 512],
                                           start=(sg == 0), stop=(sg == 3),
                                           skip_group_check=True),
                                  "PE" if (sg == 3) else None)
        for d in range(2, ND):
            w("scalar", "PE", pe_scan[I][d])
            dv_ztevac[I][d] = op("scalar", lambda e, I=I, d=d:
                                 e.activation(ZT[d][:, 512 * I:512 * I + 512], PB[d][:, 0:512],
                                              AF.Identity, bias=0.0, scale=1.0), "SC")
        if I < NCH - 1:
            for t_ in range(4 * (I + 1), 4 * (I + 2)):
                emit_x1_norm(t_)
        if I < NCH - 1:
            w("vector", "PE", pe_mv[I])
            w("vector", "DV", dv_init if I == 0 else dv_state[I - 1])
            op("vector", lambda e: e.scalar_tensor_tensor(
                Srowb[0:1, 0:512], Srowb[0:1, 0:512], D512[0:1, 0:1],
                PB[0][0:1, 0:512], ALU.mult, ALU.add))
            dv_state[I] = op("vector", lambda e: e.scalar_tensor_tensor(
                Srowb[0:1, 512:1024], Srowb[0:1, 512:1024], D512[0:1, 0:1],
                PB[1][0:1, 0:512], ALU.mult, ALU.add), "DV")

    if phases <= 2:
        return finish_debug(pb, at)

    INVD = 1.0 / 1024.0

    def lnacc_s1(PSRC, Tb, c, wait_pe, t_free_dv):
        w("scalar", "PE", wait_pe)
        w("scalar", "DV", t_free_dv)
        return op("scalar", lambda e: e.activation(
            Tb[:], PSRC[:], AF.Identity, bias=0.0, scale=1.0,
            accum_out=SCR[:, c:c + 1]), "SC")

    def lnacc_s2(Tb, SQb, c, sc_cp):
        w("vector", "SC", sc_cp)
        op("vector", lambda e: e.tensor_scalar(SCR[:, c:c + 1], SCR[:, c:c + 1], INVD, None, ALU.mult))
        v = op("vector", lambda e: e.scalar_tensor_tensor(
            SQb[:], Tb[:], 1.0, Tb[:], ALU.mult, ALU.mult, accum_out=SCR[:, c + 1:c + 2]), "DV")
        w("vector", "DV", v)
        v = op("vector", lambda e: e.scalar_tensor_tensor(
            SCR[:, c + 2:c + 3], SCR[:, c:c + 1], 1.0, SCR[:, c:c + 1], ALU.mult, ALU.mult), "DV")
        w("vector", "DV", v)
        dv_v = op("vector", lambda e: e.tensor_scalar(
            SCR[:, c + 3:c + 4], SCR[:, c + 1:c + 2], INVD, SCR[:, c + 2:c + 3],
            ALU.mult, ALU.subtract), "DV")
        w("scalar", "DV", dv_v)
        sc_sq = op("scalar", lambda e: e.activation(
            SCR[:, c + 4:c + 5], SCR[:, c + 3:c + 4], AF.Sqrt, bias=EPSc[:], scale=1.0), "SC")
        return sc_sq

    def lnacc_s3(Tb, c, Xt, sc_sq, x_war_dv):
        w("vector", "SC", sc_sq)
        v = op("vector", lambda e: e.reciprocal(SCR[:, c + 4:c + 5], SCR[:, c + 4:c + 5]), "DV")
        w("vector", "DV", v)
        dv_nb = op("vector", lambda e: e.scalar_tensor_tensor(
            SCR[:, c + 5:c + 6], SCR[:, c:c + 1], -1.0, SCR[:, c + 4:c + 5], ALU.mult, ALU.mult), "DV")
        w("scalar", "DV", dv_nb)
        sc_nrm = op("scalar", lambda e: e.activation(
            Tb[:], Tb[:], AF.Identity, bias=SCR[:, c + 5:c + 6], scale=SCR[:, c + 4:c + 5]), "SC")
        w("vector", "SC", sc_nrm)
        w("vector", "DV", x_war_dv)
        dv_xadd = op("vector", lambda e: e.tensor_tensor(Xt[:], Xt[:], Tb[:], ALU.add), "DV")
        return sc_nrm, dv_xadd

    pb.close()
    pc = ExitStack()
    PR = [pc.enter_context(nc.psum_tensor(f"pr{i}", [128, 1024], F32)) for i in range(4)]

    pe_retr = [0] * TN
    sc_cp = [0] * TN
    sc_sqa = [0] * TN
    dv_nrm_a = [0] * TN
    dv_xattn = [0] * TN

    def attn_stage1(t):
        p, pr = t % 4, t % 2
        return lnacc_s1(PR[p], T1A[pr], 6 * pr,
                        wait_pe=pe_retr[t],
                        t_free_dv=dv_xattn[t - 2] if t >= 2 else 0)

    def attn_stage2(t):
        pr = t % 2
        return lnacc_s2(T1A[pr], SQD[pr], 6 * pr, sc_cp[t])

    def attn_stage3(t):
        pr = t % 2
        nrm, xadd = lnacc_s3(T1A[pr], 6 * pr, X[t], sc_sqa[t], 0)
        dv_nrm_a[t] = nrm
        dv_xattn[t] = xadd

    for t in range(TN):
        p = t % 4
        w("tensor", "SC", dv_ztevac[NCH - 1][7])
        w("tensor", "DV", dv_hcg)
        if t >= 4:
            w("tensor", "SC", sc_cp[t - 4])
        for fc in range(2):
            for d in range(ND):
                op("tensor", lambda e, t=t, p=p, fc=fc, d=d:
                   e.matmul(PR[p][:, 512 * fc:512 * fc + 512],
                            ZT[d][:, 128 * t:128 * t + 128],
                            HC[d][:, 512 * fc:512 * fc + 512],
                            start=(d == 0), stop=False, skip_group_check=True))
            pe_retr[t] = op("tensor", lambda e, t=t, p=p, fc=fc:
                            e.matmul(PR[p][:, 512 * fc:512 * fc + 512], IDB[:],
                                     XW[t][:, 512 * fc:512 * fc + 512],
                                     start=False, stop=True, skip_group_check=True), "PE")
        if t >= 2:
            attn_stage3(t - 2)
        if t >= 1:
            sc_sqa[t - 1] = attn_stage2(t - 1)
        sc_cp[t] = attn_stage1(t)
    sc_sqa[TN - 1] = attn_stage2(TN - 1)
    attn_stage3(TN - 2)
    attn_stage3(TN - 1)
    if phases < 90:
        for i, t in enumerate(DUMP_TILES):
            w("vector", "DV", dv_xattn[t])
            op("vector", lambda e, i=i, c=6 * (t % 2): e.tensor_copy(STDMP[i][:, 0:6], SCR[:, c:c + 6]))

    if phases <= 3:
        return finish_debug(pc, at, dump=[(zt_o, ZT), (hc_o, HC), (xw_o, XW), (st_o, STDMP)])

    ffps_pr0 = PR[0]
    pc.close()
    at.close()

    ff = ExitStack()
    fsb = lambda n, shp, dt: ff.enter_context(nc.sbuf_tensor(n, shp, dt))
    X2T = fsb("x2t", [128, 8192], BF16)
    X2S = [fsb(f"x2s{i}", [128, 1024], BF16) for i in range(2)]
    SIL = [fsb(f"sil{i}", [128, 1024], BF16) for i in range(2)]
    T1FA = [fsb(f"t1f{i}", [128, 1024], F32) for i in range(2)]
    SQDF1 = fsb("sqdf", [128, 1024], BF16)
    SQDF = [SQDF1, SQDF1]
    WGU = [fsb(f"wgu{i}", [128, 2048], BF16) for i in range(2)]

    ffps = {"PR0": ffps_pr0}
    prev_dn_sc = 0
    slot_pe = [0, 0]
    prev_half_sc = sc_cp[15]
    prev_half_dv = dv_xattn[15]

    w("gpsimd", "PE", pe_retr[15])
    w("gpsimd", "DV", dv_xattn[15])
    w("gpsimd", "SC", sc_cp[15])
    w("scalar", "PE", pe_retr[15])
    w("scalar", "DV", dv_xattn[15])
    w("vector", "PE", pe_retr[15])
    w("vector", "SC", sc_cp[15])
    dv_xfin = [0] * TN

    def emit_ln2(hf, war_dv):
        tg0 = 8 * hf
        w("vector", "DV", war_dv)
        for tl in range(8):
            t = tg0 + tl
            p = 12 * (tl % 2)
            w("vector", "DV", dv_xattn[t])
            if tl >= 2:
                w("vector", "DV", agg_fin[tl - 2])
            op("vector", lambda e, t=t, p=p: e.bn_stats(STA[:, p:p + 6], X[t][:, 0:512]))
            stat_fin[tl] = op("vector", lambda e, t=t, p=p: e.bn_stats(STA[:, p + 6:p + 12], X[t][:, 512:1024]), "DV")
            if tl >= 1:
                q = 12 * ((tl - 1) % 2)
                w("vector", "DV", stat_fin[tl - 1])
                op("vector", lambda e, t=t - 1, q=q: e.bn_aggr(aggr_out(MV1, t), STA[:, q:q + 6]))
                gap("vector", 64)
                agg_fin[tl - 1] = op("vector", lambda e, t=t - 1, q=q: e.bn_aggr(aggr_out(MV2, t), STA[:, q + 6:q + 12]), "DV")
        w("vector", "DV", stat_fin[7])
        q = 12 * (7 % 2)
        op("vector", lambda e, t=tg0 + 7, q=q: e.bn_aggr(aggr_out(MV1, t), STA[:, q:q + 6]))
        gap("vector", 64)
        fin = op("vector", lambda e, t=tg0 + 7, q=q: e.bn_aggr(aggr_out(MV2, t), STA[:, q + 6:q + 12]), "DV")
        w("vector", "DV", fin)
        return ln_combine(tg0, 8, M1c, R2s, NB2s, fold=INV127)

    def emit_x2pipe(hf, dv_ln2, psum_sc, psum_dv, prev_pe_tr):
        tg0 = 8 * hf
        PT = ffps["PT"]
        sc_x2 = [0] * 8
        pe_tr = [0] * 8
        dv_pt = [0] * 8
        for tl in range(8):
            t = tg0 + tl
            w("scalar", "DV", dv_ln2)
            w("scalar", "PE", pe_tr[tl - 2] if tl >= 2 else prev_pe_tr[tl - 2])
            sc_x2[tl] = op("scalar", lambda e, t=t, tl=tl: e.activation(
                X2S[tl % 2][:], X[t][:], AF.Identity,
                bias=NB2s[:, t:t + 1], scale=R2s[:, t:t + 1]), "SC")
            w("tensor", "SC", sc_x2[tl])
            if tl == 0:
                w("tensor", "SC", psum_sc)
                w("tensor", "DV", psum_dv)
            else:
                w("tensor", "DV", dv_pt[tl - 1])
            for d in range(ND):
                pe_tr[tl] = op("tensor", lambda e, tl=tl, d=d: e.transpose(
                    PT[:, 128 * d:128 * d + 128], X2S[tl % 2][:, 128 * d:128 * d + 128], IDB[:]),
                    "PE" if d == ND - 1 else None)
            w("vector", "PE", pe_tr[tl])
            dv_pt[tl] = op("vector", lambda e, tl=tl: e.tensor_copy(
                X2T[...].rearrange("p (d t) -> p d t", d=ND)[:, :, 128 * tl:128 * tl + 128],
                PT[...].rearrange("p (d t) -> p d t", d=ND)[:, :, :]), "DV")
        return sc_x2, pe_tr, dv_pt

    pt_scope = ExitStack()
    ffps["PT"] = pt_scope.enter_context(nc.psum_tensor("pt", [128, 1024], BF16))
    pt_scope.close()

    dv_ln2_done = 0
    x2pipe_B = None

    for hf in range(2):
        tg0 = 8 * hf

        if hf == 0:
            dv_ln2_done = emit_ln2(0, 0)
            for wv in (stat_fin[1], stat_fin[3], stat_fin[5], stat_fin[7],
                       agg_fin[6], dv_ln2_done):
                w("tensor", "DV", wv)
                op("tensor", lambda e: e.matmul(
                    ffps["PR0"][0:64, 0:64], XW[0][:, 0:64], XW[0][:, 0:64],
                    start=True, stop=True, skip_group_check=True))
            sc_x2, pe_tr, dv_pt = emit_x2pipe(0, dv_ln2_done, prev_half_sc, prev_half_dv,
                                              [pe_retr[15]] * 2)
        else:
            sc_x2, pe_tr, dv_pt = x2pipe_B
        if phases <= 4:
            return finish_debug(ff)

        if hf == 0:
            pg_scope = ExitStack()
            PG = [pg_scope.enter_context(nc.psum_tensor(f"pg{i}", [128, 1024], F32)) for i in range(2)]
            PU = [pg_scope.enter_context(nc.psum_tensor(f"pu{i}", [128, 1024], F32)) for i in range(2)]
            ffps["PG"], ffps["PU"] = PG, PU
        else:
            PG, PU = ffps["PG"], ffps["PU"]
        wt_gu = [0] * NH
        pe_g = [[0] * 2 for _ in range(NH)]
        pe_u = [[0] * 2 for _ in range(NH)]
        sc_sil = [[0] * 2 for _ in range(NH)]
        dv_ht = [[0] * 2 for _ in range(NH)]

        def HT(eta):
            return XW[eta] if eta < 16 else HC[eta - 16]

        for eta in range(NH):
            sl = eta % 2
            if hf == 1 and eta == 0:
                w("tensor", "SC", prev_dn_sc)
            w("gpsimd", "PE", pe_u[eta - 2][1] if eta >= 2 else slot_pe[sl])
            wt_gu[eta] = dma("gpsimd", WGU[sl][:], wgu[eta, :, :], f"WG{sl}")
            w("tensor", f"WG{sl}", wt_gu[eta])
            w("tensor", "DV", dv_pt[7])
            if eta >= 2:
                w("tensor", "SC", sc_sil[eta - 2][1])
                w("tensor", "DV", dv_ht[eta - 2][1])
            for c in range(2):
                for d in range(ND):
                    pe_g[eta][c] = op("tensor", lambda e, sl=sl, c=c, d=d: e.matmul(
                        PG[sl][:, 512 * c:512 * c + 512],
                        WGU[sl][:, 128 * d:128 * d + 128],
                        X2T[:, 1024 * d + 512 * c:1024 * d + 512 * c + 512],
                        start=(d == 0), stop=(d == ND - 1)),
                        "PE" if d == ND - 1 else None)
            for c in range(2):
                for d in range(ND):
                    pe_u[eta][c] = op("tensor", lambda e, sl=sl, c=c, d=d: e.matmul(
                        PU[sl][:, 512 * c:512 * c + 512],
                        WGU[sl][:, 1024 + 128 * d:1024 + 128 * d + 128],
                        X2T[:, 1024 * d + 512 * c:1024 * d + 512 * c + 512],
                        start=(d == 0), stop=(d == ND - 1)),
                        "PE" if d == ND - 1 else None)
            slot_pe[sl] = pe_u[eta][1]
            if hf == 0 and eta == 6:
                dv_ln2_done = emit_ln2(1, dv_ln2_done)
            for c in range(2):
                w("scalar", "PE", pe_g[eta][c])
                if eta >= 2:
                    w("scalar", "DV", dv_ht[eta - 2][c])
                sc_sil[eta][c] = op("scalar", lambda e, sl=sl, c=c: e.activation(
                    SIL[sl][:, 512 * c:512 * c + 512], PG[sl][:, 512 * c:512 * c + 512],
                    AF.Silu, bias=Z128[:], scale=1.0), "SC")
            for c in range(2):
                w("vector", "SC", sc_sil[eta][c])
                w("vector", "PE", pe_u[eta][c])
                dv_ht[eta][c] = op("vector", lambda e, eta=eta, sl=sl, c=c: e.scalar_tensor_tensor(
                    HT(eta)[:, 512 * c:512 * c + 512], SIL[sl][:, 512 * c:512 * c + 512],
                    INV127, PU[sl][:, 512 * c:512 * c + 512], ALU.mult, ALU.mult), "DV")
        if hf == 0:
            pg_scope.close()
        if phases <= 5:
            return finish_debug(ff)

        if hf == 0:
            pf_scope = ExitStack()
            PF = [pf_scope.enter_context(nc.psum_tensor(f"pf{i}", [128, 1024], F32)) for i in range(2)]
            ffps["PF"] = PF
        else:
            PF = ffps["PF"]
        PFS = [PF[0], PF[1], PU[0], PU[1]]
        pe_dn = [0] * 8
        sc_cpf = [0] * 8
        sc_sqf = [0] * 8
        dv_nrmf = [0] * 8

        def dn_stage1(tl):
            pr = tl % 2
            return lnacc_s1(PFS[tl % 4], T1FA[pr], 6 * pr,
                            wait_pe=pe_dn[tl],
                            t_free_dv=dv_xfin[tg0 + tl - 2] if tl >= 2 else
                            (dv_xfin[7] if hf == 1 else 0))

        def dn_stage2(tl):
            pr = tl % 2
            return lnacc_s2(T1FA[pr], SQDF[pr], 6 * pr, sc_cpf[tl])

        def dn_stage3(tl):
            t = tg0 + tl
            pr = tl % 2
            nrm, xadd = lnacc_s3(T1FA[pr], 6 * pr, X[t], sc_sqf[tl], dv_xattn[t])
            dv_nrmf[tl] = nrm
            dv_xfin[t] = xadd
            w("sync", "DV", dv_xfin[t])
            dma("sync", out[t, :, :], X[t][:], "ST")

        w("tensor", "WWD", wt_wd)
        w("tensor", "DV", dv_ht[NH - 1][1])
        w("tensor", "SC", sc_sil[NH - 1][1])
        for tl in range(8):
            t = tg0 + tl
            sl = tl % 4
            if tl >= 4:
                w("tensor", "SC", sc_cpf[tl - 4])
            for fc in range(2):
                for eta in range(NH):
                    pe_dn[tl] = op("tensor", lambda e, sl=sl, tl=tl, fc=fc, eta=eta: e.matmul(
                        PFS[sl][:, 512 * fc:512 * fc + 512],
                        HT(eta)[:, 128 * tl:128 * tl + 128],
                        WDTB[:, 1024 * eta + 512 * fc:1024 * eta + 512 * fc + 512],
                        start=(eta == 0), stop=(eta == NH - 1)),
                        "PE" if (eta == NH - 1 and fc == 1) else None)
            if tl >= 2:
                dn_stage3(tl - 2)
            if tl >= 1:
                sc_sqf[tl - 1] = dn_stage2(tl - 1)
            sc_cpf[tl] = dn_stage1(tl)
            if hf == 0 and tl == 5:
                x2pipe_B = emit_x2pipe(1, dv_ln2_done, sc_cpf[4], 0, pe_tr)
        sc_sqf[7] = dn_stage2(7)
        dn_stage3(6)
        dn_stage3(7)
        if hf == 0:
            pf_scope.close()
        if phases <= 6 and hf == 0:
            return finish_debug(ff)
        prev_dn_sc = sc_cpf[7]

    ff.close()

    w("sync", "ST", C["ST"])

    with nc.Block() as block:
        @block.sync
        def _(e):
            for f in plan["sync"]:
                f(e)

        @block.gpsimd
        def _(e):
            for f in plan["gpsimd"]:
                f(e)

        @block.vector
        def _(e):
            for f in plan["vector"]:
                f(e)

        @block.scalar
        def _(e):
            for f in plan["scalar"]:
                f(e)

        @block.tensor
        def _(e):
            for f in plan["tensor"]:
                f(e)

    es.close()
    return nc


def kernel(L0_attn_x, L0_attn_bind_key, L0_attn_unbind_key, L0_attn_gate, L0_attn_decay,
           L0_ffn_w_gate, L0_ffn_w_up, L0_ffn_w_down):
    x = np.ascontiguousarray(np.asarray(L0_attn_x), dtype=np.float32)
    b = np.asarray(L0_attn_bind_key)
    u = np.asarray(L0_attn_unbind_key)
    gate = np.asarray(L0_attn_gate).reshape(1)
    decay = np.asarray(L0_attn_decay).reshape(())
    wg = np.asarray(L0_ffn_w_gate)
    wu = np.asarray(L0_ffn_w_up)
    wd = np.asarray(L0_ffn_w_down)

    D = 1024
    idx = np.arange(D)
    circ = (idx[:, None] - idx[None, :]) % D
    cbt = np.ascontiguousarray(b[circ].reshape(ND, 128, D).transpose(1, 0, 2).reshape(128, ND * D))
    cuT = np.ascontiguousarray(u[circ].reshape(ND, 128, D).transpose(1, 0, 2).reshape(128, ND * D))

    ii = np.arange(128)[:, None]
    jj = np.arange(512)[None, :]
    e0m = np.maximum(jj - ii, 0).astype(np.float32)
    cstf = np.ascontiguousarray(np.concatenate(
        [e0m, np.broadcast_to(decay, (128, 1)).astype(np.float32),
         np.broadcast_to(gate, (128, 1)).astype(np.float32)], axis=1))
    cstb = np.ascontiguousarray(np.concatenate(
        [(jj >= ii).astype(ml_dtypes.bfloat16), np.eye(128, dtype=ml_dtypes.bfloat16)], axis=1))

    wgt = wg.reshape(NH, 128, ND, 128).transpose(0, 3, 2, 1).reshape(NH, 128, 1024)
    wut = wu.reshape(NH, 128, ND, 128).transpose(0, 3, 2, 1).reshape(NH, 128, 1024)
    wgu_h = np.ascontiguousarray(np.concatenate([wgt, wut], axis=2))
    wdt = np.ascontiguousarray(wd.T.reshape(NH, 128, 1024).transpose(1, 0, 2).reshape(128, NH * 1024))

    nc = build_nc()
    in_maps = []
    for c in range(8):
        in_maps.append(dict(
            xin=np.ascontiguousarray(x[c].reshape(TN, 128, 1024).transpose(1, 0, 2).reshape(128, TN * 1024)),
            cbt=cbt, cuT=cuT, cstf=cstf, cstb=cstb,
            wgu=wgu_h, wdt=wdt,
        ))
    res = run_bass_kernel_spmd(nc, in_maps, core_ids=list(range(8)))
    return np.stack([res.results[c]["out"].reshape(2048, 1024) for c in range(8)], axis=0)


# version 21 (final)
# speedup vs baseline: 1.0592x; 1.0185x over previous
import numpy as np
import ml_dtypes
from contextlib import ExitStack

import concourse.bass as bass
import concourse.mybir as mybir
from concourse.bass_utils import run_bass_kernel_spmd

F32 = mybir.dt.float32
BF16 = mybir.dt.bfloat16
I32 = mybir.dt.int32
AF = mybir.ActivationFunctionType
ALU = mybir.AluOpType

TN = 16
ND = 8
NH = 24
NCH = 4
INV127 = 1.0 / 127.0


def build_nc(phases=99):
    nc = bass.Bass("TRN2", target_bir_lowering=False)

    xin = nc.dram_tensor("xin", [128, TN * 1024], F32, kind="ExternalInput")
    cbt = nc.dram_tensor("cbt", [128, ND * 1024], F32, kind="ExternalInput")
    cuT = nc.dram_tensor("cuT", [128, ND * 1024], F32, kind="ExternalInput")
    cstf = nc.dram_tensor("cstf", [128, 514], F32, kind="ExternalInput")
    cstb = nc.dram_tensor("cstb", [128, 640], BF16, kind="ExternalInput")
    wgu = nc.dram_tensor("wgu", [NH, 128, 2048], I32, kind="ExternalInput")
    wdt = nc.dram_tensor("wdt", [128, NH * 1024], I32, kind="ExternalInput")
    out = nc.dram_tensor("out", [TN, 128, 1024], F32, kind="ExternalOutput")
    if phases < 90:
        zt_o = nc.dram_tensor("zt_o", [ND, 128, 2048], BF16, kind="ExternalOutput")
        hc_o = nc.dram_tensor("hc_o", [ND, 128, 1024], BF16, kind="ExternalOutput")
        xw_o = nc.dram_tensor("xw_o", [TN, 128, 1024], BF16, kind="ExternalOutput")
        st_o = nc.dram_tensor("st_o", [2, 128, 8], F32, kind="ExternalOutput")

    plan = {k: [] for k in ("sync", "gpsimd", "vector", "scalar", "tensor")}
    C = {"LDX": 0, "LDX2": 0, "LDX3": 0, "LDX4": 0, "LDC": 0, "WCC": 0, "WWD": 0,
         "WG0": 0, "WG1": 0, "XA0": 0, "XA1": 0, "XB0": 0, "XB1": 0,
         "PE": 0, "DV": 0, "SC": 0, "GP": 0, "ST": 0}
    S = {}

    def w(eng, sem, val):
        if val and val > 0:
            plan[eng].append(lambda e, s=sem, v=val: e.wait_ge(S[s], v))

    _gapn = {"vector": 0, "scalar": 0}

    def gap(eng, cyc=300):
        n = max(1, cyc // 200)
        for _ in range(n):
            i = _gapn[eng] % 3
            _gapn[eng] += 1
            if eng == "vector":
                plan[eng].append(lambda e, i=i: e.tensor_copy(GAPT[:, i:i + 1], Z128[:]))
            else:
                plan[eng].append(lambda e, i=i: e.activation(GAPS[:, i:i + 1], Z128[:],
                                                             AF.Identity, bias=0.0, scale=1.0))

    def dma(eng, out_ap, in_ap, sem):
        C[sem] += 16
        plan[eng].append(lambda e, o=out_ap, i=in_ap, s=sem:
                         e.dma_start(out=o, in_=i).then_inc(S[s], 16))
        return C[sem]

    def op(eng, fn, sem=None):
        if sem is None:
            plan[eng].append(fn)
            return None
        C[sem] += 1
        plan[eng].append(lambda e, f=fn, s=sem: f(e).then_inc(S[s], 1))
        return C[sem]

    es = ExitStack()
    for name in C:
        S[name] = es.enter_context(nc.semaphore(name))

    sb = lambda n, shp, dt: es.enter_context(nc.sbuf_tensor(n, shp, dt))

    X = [sb(f"x{t}", [128, 1024], F32) for t in range(TN)]
    XW = [sb(f"xw{t}", [128, 1024], BF16) for t in range(TN)]
    HC = [sb(f"hc{j}", [128, 1024], BF16) for j in range(ND)]
    WDTB = sb("wdtb", [128, NH * 1024], BF16)
    IDBT = sb("idb_s", [128, 128], BF16)
    M1c = sb("m1c", [128, 16], F32)
    R1c = sb("r1c", [128, 16], F32)
    NB1c = sb("nb1c", [128, 16], F32)
    R2s = sb("r2s", [128, 16], F32)
    NB2s = sb("nb2s", [128, 16], F32)
    STA = sb("sta", [128, 24], F32)
    MV1 = sb("mv1", [128, 32], F32)
    MV2 = sb("mv2", [128, 32], F32)
    WS = [sb(f"ws{i}", [128, 16], F32) for i in range(6)]
    SCR = sb("scr", [128, 12], F32)
    LMV = sb("lmv", [128, 8], F32)
    LSTA = sb("lsta", [128, 24], F32)
    EPSc = sb("epsc", [128, 1], F32)
    Z128 = sb("z128", [128, 1], F32)
    SIGc = sb("sigc", [128, 1], F32)
    LNDc = sb("lndc", [128, 1], F32)
    GAPT = sb("gapt", [128, 3], F32)
    GAPS = sb("gaps", [128, 3], F32)

    def mv_m(MV, lo, n):
        return MV[:, 0 + lo: 0 + lo + n]

    def mv_v(MV, lo, n):
        return MV[:, 16 + lo: 16 + lo + n]

    def aggr_out(MV, t):
        return MV[:, :].rearrange("p (two t) -> p two t", two=2)[:, :, t:t + 1]

    at = ExitStack()
    asb = lambda n, shp, dt: at.enter_context(nc.sbuf_tensor(n, shp, dt))
    L0B = asb("l0b", [128, 512], BF16)
    DPb = asb("dpb", [1, 512], BF16)
    D512 = asb("d512", [1, 1], F32)
    WCf = asb("wcf", [128, 4], F32)
    WCb = asb("wcb", [128, 4], BF16)
    Srowb = asb("srowb", [1, 1024], BF16)

    a2 = ExitStack()
    a2sb = lambda n, shp, dt: a2.enter_context(nc.sbuf_tensor(n, shp, dt))
    CBTB = a2sb("cbtb", [128, ND * 1024], BF16)
    CUBB = a2sb("cubb", [128, ND * 1024], BF16)
    L0F = a2sb("l0f", [128, 512], BF16)
    CSTF = a2sb("cstf_s", [128, 514], F32)
    CSTB = a2sb("cstb_s", [128, 512], BF16)

    class _CV:
        def __init__(self, T, f0, fN):
            self.T, self.f0, self.fN = T, f0, fN

        def __getitem__(self, key):
            if key == slice(None):
                return self.T[:, self.f0:self.f0 + self.fN]
            p, f = key
            fs = self.f0 + (f.start or 0)
            fe = self.f0 + (f.stop if f.stop is not None else self.fN)
            return self.T[p, fs:fe]

    E0M = _CV(CSTF, 0, 512)
    DECB = _CV(CSTF, 512, 1)
    GATB = _CV(CSTF, 513, 1)
    M0B = _CV(CSTB, 0, 512)
    IDB = IDBT

    dma("gpsimd", CSTF[:], cstf[:, :], "LDC")
    dma("gpsimd", CSTB[:], cstb[:, 0:512], "LDC")
    ld_consts = dma("gpsimd", IDBT[:], cstb[:, 512:640], "LDC")
    for t in range(TN):
        dma("sync", X[t][:], xin[:, 1024 * t:1024 * t + 1024], f"LDX{t // 4 + 1}".replace("LDX1", "LDX"))
    ldx_tot = [4 * 16] * 4

    dma("gpsimd", CBTB[:], cbt[:, :], "WCC")
    wt_ccin = dma("gpsimd", CUBB[:], cuT[:, :], "WCC")
    for gname in ("LDX", "LDX2", "LDX3", "LDX4"):
        w("gpsimd", gname, 64)
    wt_wd = dma("gpsimd", WDTB[:], wdt[:, :], "WWD")

    op("vector", lambda e: e.memset(GAPT[:], 0.0))
    op("vector", lambda e: e.memset(Z128[:], 0.0))
    op("vector", lambda e: e.memset(EPSc[:], 1e-5))
    dv_init = op("vector", lambda e: e.memset(Srowb[:], 0.0), "DV")

    def finish_debug(*inner_stacks, dump=()):
        w("sync", "DV", C["DV"])
        w("sync", "SC", C["SC"])
        w("sync", "PE", C["PE"])
        for snm in ("WCC", "WWD", "WG0", "WG1", "XA0", "XA1", "XB0", "XB1"):
            w("sync", snm, C[snm])
        for t in range(TN):
            dma("sync", out[t, :, :], X[t][:], "ST")
        for dst, tiles in dump:
            for i, tile in enumerate(tiles):
                dma("sync", dst[i, :, :], tile[:], "ST")
        w("sync", "ST", C["ST"])
        with nc.Block() as block:
            block.sync(lambda e: [f(e) for f in plan["sync"]] and None)
            block.gpsimd(lambda e: [f(e) for f in plan["gpsimd"]] and None)
            block.vector(lambda e: [f(e) for f in plan["vector"]] and None)
            block.scalar(lambda e: [f(e) for f in plan["scalar"]] and None)
            block.tensor(lambda e: [f(e) for f in plan["tensor"]] and None)
        for st in inner_stacks:
            st.close()
        es.close()
        return nc

    if phases <= 0.25:
        return finish_debug(a2, at)

    w("scalar", "LDC", ld_consts)
    w("scalar", "DV", dv_init)
    sc_sig = op("scalar", lambda e: e.activation(SIGc[:], DECB[:], AF.Sigmoid,
                                                 bias=Z128[:], scale=1.0), "SC")
    w("vector", "SC", sc_sig)
    dv_sig = op("vector", lambda e: e.tensor_scalar(SIGc[:], SIGc[:], 1e-38, None, ALU.max), "DV")
    w("scalar", "DV", dv_sig)
    scv = op("scalar", lambda e: e.activation(LNDc[:], SIGc[:], AF.Ln, bias=Z128[:], scale=1.0), "SC")
    w("scalar", "SC", scv)
    op("scalar", lambda e: e.activation(L0F[:], E0M[:], AF.Exp, bias=Z128[:], scale=LNDc[:]))
    op("scalar", lambda e: e.activation(DPb[:], E0M[0:1, :], AF.Exp,
                                        bias=LNDc[0:1, :], scale=LNDc[0:1, :]))
    op("scalar", lambda e: e.activation(D512[:], E0M[0:1, 511:512], AF.Exp,
                                        bias=LNDc[0:1, :], scale=LNDc[0:1, :]))
    sc_lnd = 0
    for sg in range(4):
        j0 = 511 - 128 * sg
        sc_lnd = op("scalar", lambda e, sg=sg, j0=j0:
                    e.activation(WCf[:, sg:sg + 1], E0M[:, j0:j0 + 1], AF.Exp,
                                 bias=Z128[:], scale=LNDc[:]), "SC")

    w("vector", "SC", sc_lnd)
    op("vector", lambda e: e.tensor_tensor(L0B[:], L0F[:], M0B[:], ALU.mult))
    dv_lprep = op("vector", lambda e: e.tensor_copy(WCb[:], WCf[:]), "DV")

    if phases <= 0.5:
        return finish_debug(a2, at)

    pa = ExitStack()
    CP = [pa.enter_context(nc.psum_tensor(f"cp{i}", [128, 512], F32)) for i in range(4)]

    w("tensor", "WCC", wt_ccin)
    pe_cc = [0] * 16
    dv_ccevac = [0] * 16
    stat_fin = [0] * TN
    agg_fin = [0] * TN
    last_chain_end = {0: 0, 1: 0}
    for g in range(16):
        dj, fc = g // 2, g % 2
        if g >= 4:
            w("tensor", "SC", dv_ccevac[g - 4])
        for k in range(ND):
            pe_cc[g] = op("tensor", lambda e, dj=dj, fc=fc, k=k, g=g:
                          e.matmul(CP[g % 4][:], CBTB[:, 1024 * k + 128 * dj:1024 * k + 128 * dj + 128],
                                   CUBB[:, 1024 * k + 512 * fc:1024 * k + 512 * fc + 512],
                                   start=(k == 0), stop=(k == ND - 1)),
                          "PE" if k == ND - 1 else None)
        w("scalar", "PE", pe_cc[g])
        dv_ccevac[g] = op("scalar", lambda e, dj=dj, fc=fc, g=g:
                          e.activation(HC[dj][:, 512 * fc:512 * fc + 512], CP[g % 4][:],
                                       AF.Identity, bias=0.0, scale=1.0), "SC")
        if phases <= 0.77:
            continue
        t = g
        if t % 4 == 0:
            w("vector", f"LDX{t // 4 + 1}".replace("LDX1", "LDX"), 64)
        p = 12 * (t % 2)
        if t >= 2:
            w("vector", "DV", agg_fin[t - 2])
        op("vector", lambda e, t=t, p=p: e.bn_stats(STA[:, p:p + 6], X[t][:, 0:512]))
        stat_fin[t] = op("vector", lambda e, t=t, p=p: e.bn_stats(STA[:, p + 6:p + 12], X[t][:, 512:1024]), "DV")
        if t >= 1:
            q = 12 * ((t - 1) % 2)
            w("vector", "DV", stat_fin[t - 1])
            op("vector", lambda e, t=t, q=q: e.bn_aggr(aggr_out(MV1, t - 1), STA[:, q:q + 6]))
            gap("vector", 64)
            agg_fin[t - 1] = op("vector", lambda e, t=t, q=q: e.bn_aggr(aggr_out(MV2, t - 1), STA[:, q + 6:q + 12]), "DV")
    if phases <= 0.8:
        return finish_debug(pa, a2, at)
    w("vector", "DV", stat_fin[15])
    q = 12 * (15 % 2)
    op("vector", lambda e, q=q: e.bn_aggr(aggr_out(MV1, 15), STA[:, q:q + 6]))
    gap("vector", 64)
    fin = op("vector", lambda e, q=q: e.bn_aggr(aggr_out(MV2, 15), STA[:, q + 6:q + 12]), "DV")
    w("vector", "DV", fin)

    def ln_combine(n0, n, Mdst, Rdst, NBdst, fold=1.0):
        ms, ds, vs_, d2, vh = WS[0], WS[1], WS[2], WS[3], WS[4]
        sl = lambda T: T[:, n0:n0 + n]
        op("vector", lambda e: e.tensor_tensor(sl(ms), mv_m(MV1, n0, n), mv_m(MV2, n0, n), ALU.add))
        op("vector", lambda e: e.tensor_tensor(sl(ds), mv_m(MV1, n0, n), mv_m(MV2, n0, n), ALU.subtract))
        v = op("vector", lambda e: e.tensor_tensor(sl(vs_), mv_v(MV1, n0, n), mv_v(MV2, n0, n), ALU.add), "DV")
        w("vector", "DV", v)
        op("vector", lambda e: e.tensor_tensor(sl(d2), sl(ds), sl(ds), ALU.mult))
        op("vector", lambda e: e.tensor_scalar(sl(vh), sl(vs_), 0.5, None, ALU.mult))
        v = op("vector", lambda e: e.tensor_scalar(sl(Mdst), sl(ms), 0.5, None, ALU.mult), "DV")
        w("vector", "DV", v)
        dv = op("vector", lambda e: e.scalar_tensor_tensor(sl(vs_), sl(d2), 0.25, sl(vh),
                                                           ALU.mult, ALU.add), "DV")
        w("scalar", "DV", dv)
        sc = op("scalar", lambda e: e.activation(sl(WS[5]), sl(vs_), AF.Sqrt,
                                                 bias=EPSc[:], scale=1.0), "SC")
        w("vector", "SC", sc)
        v = op("vector", lambda e: e.reciprocal(sl(Rdst), sl(WS[5])), "DV")
        w("vector", "DV", v)
        if fold != 1.0:
            v = op("vector", lambda e: e.tensor_scalar(sl(Rdst), sl(Rdst), fold, None, ALU.mult), "DV")
            w("vector", "DV", v)
        return op("vector", lambda e: e.scalar_tensor_tensor(sl(NBdst), sl(Mdst), -1.0, sl(Rdst),
                                                             ALU.mult, ALU.mult), "DV")

    if phases <= 0.85:
        return finish_debug(pa, a2, at)

    dv_ln1 = ln_combine(0, 16, M1c, R1c, NB1c)

    gap("vector")
    dv_hcg = 0
    w("vector", "LDC", ld_consts)
    w("vector", "SC", dv_ccevac[15])
    for j in range(ND):
        dv_hcg = op("vector", lambda e, j=j:
                    e.tensor_scalar(HC[j][:], HC[j][:], GATB[:], None, ALU.mult), "DV")

    if phases <= 0.9:
        return finish_debug(pa, a2, at)

    sc_x1 = [0] * TN

    def emit_x1_norm(t):
        w("vector", "DV", dv_ln1)
        sc_x1[t] = op("vector", lambda e, t=t:
                      e.tensor_scalar(XW[t][:], X[t][:], R1c[:, t:t + 1], NB1c[:, t:t + 1],
                                      ALU.mult, ALU.add), "DV")

    for t in range(4):
        emit_x1_norm(t)

    pa.close()
    pb = ExitStack()
    PB = [pb.enter_context(nc.psum_tensor(f"pb{i}", [128, 512], F32)) for i in range(8)]

    pe_scan = [[0] * 8 for _ in range(NCH)]
    pe_mv = [0] * NCH
    dv_ztevac = [[0] * 8 for _ in range(NCH)]
    dv_state = [0] * NCH

    if phases <= 1:
        return finish_debug(pa, a2, at)

    w("tensor", "DV", dv_lprep)
    w("tensor", "SC", dv_ccevac[15])

    a2.close()
    DUMP_TILES = (0, 15)
    STDMP = [at.enter_context(nc.sbuf_tensor(f"stdmp{t}", [128, 8], F32)) for t in range(2)] if phases < 90 else None
    ZT = [at.enter_context(nc.sbuf_tensor(f"zt{d}", [128, 2048], BF16)) for d in range(ND)]
    T1A = [at.enter_context(nc.sbuf_tensor(f"t1a{i}", [128, 1024], F32)) for i in range(2)]
    SQD1 = at.enter_context(nc.sbuf_tensor("sqd", [128, 1024], BF16))
    SQD = [SQD1, SQD1]

    w("scalar", "DV", dv_hcg)
    for I in range(NCH):
        w("tensor", "DV", sc_x1[4 * I + 3])
        if I > 0:
            w("tensor", "DV", dv_state[I - 1])
        for d in range(ND):
            if I > 0:
                w("tensor", "SC", dv_ztevac[I - 1][d])
                op("tensor", lambda e, d=d:
                   e.matmul(PB[d][:, 0:512], Srowb[0:1, 128 * d:128 * d + 128], DPb[0:1, :],
                            start=True, stop=False, skip_group_check=True))
            for sg in range(4):
                last = (sg == 3)
                pe_scan[I][d] = op("tensor", lambda e, I=I, d=d, sg=sg:
                                   e.matmul(PB[d][:, 128 * sg:512],
                                            XW[4 * I + sg][:, 128 * d:128 * d + 128],
                                            L0B[:, 0:512 - 128 * sg],
                                            start=(I == 0 and sg == 0), stop=(sg == 3),
                                            skip_group_check=True),
                                   "PE" if last else None)
        for d in range(2):
            w("scalar", "PE", pe_scan[I][d])
            dv_ztevac[I][d] = op("scalar", lambda e, I=I, d=d:
                                 e.activation(ZT[d][:, 512 * I:512 * I + 512], PB[d][:, 0:512],
                                              AF.Identity, bias=0.0, scale=1.0), "SC")
        if I < NCH - 1:
            w("tensor", "SC", dv_ztevac[I][1])
            for dc in range(2):
                for sg in range(4):
                    pe_mv[I] = op("tensor", lambda e, I=I, dc=dc, sg=sg:
                                  e.matmul(PB[dc][0:1, 0:512], WCb[:, sg:sg + 1],
                                           XW[4 * I + sg][:, 512 * dc:512 * dc + 512],
                                           start=(sg == 0), stop=(sg == 3),
                                           skip_group_check=True),
                                  "PE" if (sg == 3) else None)
        for d in range(2, ND):
            w("scalar", "PE", pe_scan[I][d])
            dv_ztevac[I][d] = op("scalar", lambda e, I=I, d=d:
                                 e.activation(ZT[d][:, 512 * I:512 * I + 512], PB[d][:, 0:512],
                                              AF.Identity, bias=0.0, scale=1.0), "SC")
        if I < NCH - 1:
            for t_ in range(4 * (I + 1), 4 * (I + 2)):
                emit_x1_norm(t_)
        if I < NCH - 1:
            w("vector", "PE", pe_mv[I])
            w("vector", "DV", dv_init if I == 0 else dv_state[I - 1])
            op("vector", lambda e: e.scalar_tensor_tensor(
                Srowb[0:1, 0:512], Srowb[0:1, 0:512], D512[0:1, 0:1],
                PB[0][0:1, 0:512], ALU.mult, ALU.add))
            dv_state[I] = op("vector", lambda e: e.scalar_tensor_tensor(
                Srowb[0:1, 512:1024], Srowb[0:1, 512:1024], D512[0:1, 0:1],
                PB[1][0:1, 0:512], ALU.mult, ALU.add), "DV")

    if phases <= 2:
        return finish_debug(pb, at)

    INVD = 1.0 / 1024.0

    def lnacc_s1(PSRC, Tb, c, wait_pe, t_free_dv):
        w("scalar", "PE", wait_pe)
        w("scalar", "DV", t_free_dv)
        return op("scalar", lambda e: e.activation(
            Tb[:], PSRC[:], AF.Identity, bias=0.0, scale=1.0,
            accum_out=SCR[:, c:c + 1]), "SC")

    def lnacc_s2(Tb, SQb, c, sc_cp):
        w("vector", "SC", sc_cp)
        op("vector", lambda e: e.tensor_scalar(SCR[:, c:c + 1], SCR[:, c:c + 1], INVD, None, ALU.mult))
        v = op("vector", lambda e: e.scalar_tensor_tensor(
            SQb[:], Tb[:], 1.0, Tb[:], ALU.mult, ALU.mult, accum_out=SCR[:, c + 1:c + 2]), "DV")
        w("vector", "DV", v)
        v = op("vector", lambda e: e.scalar_tensor_tensor(
            SCR[:, c + 2:c + 3], SCR[:, c:c + 1], 1.0, SCR[:, c:c + 1], ALU.mult, ALU.mult), "DV")
        w("vector", "DV", v)
        dv_v = op("vector", lambda e: e.tensor_scalar(
            SCR[:, c + 3:c + 4], SCR[:, c + 1:c + 2], INVD, SCR[:, c + 2:c + 3],
            ALU.mult, ALU.subtract), "DV")
        w("scalar", "DV", dv_v)
        sc_sq = op("scalar", lambda e: e.activation(
            SCR[:, c + 4:c + 5], SCR[:, c + 3:c + 4], AF.Sqrt, bias=EPSc[:], scale=1.0), "SC")
        return sc_sq

    def lnacc_s3(Tb, c, Xt, sc_sq, x_war_dv):
        w("vector", "SC", sc_sq)
        v = op("vector", lambda e: e.reciprocal(SCR[:, c + 4:c + 5], SCR[:, c + 4:c + 5]), "DV")
        w("vector", "DV", v)
        dv_nb = op("vector", lambda e: e.scalar_tensor_tensor(
            SCR[:, c + 5:c + 6], SCR[:, c:c + 1], -1.0, SCR[:, c + 4:c + 5], ALU.mult, ALU.mult), "DV")
        w("scalar", "DV", dv_nb)
        sc_nrm = op("scalar", lambda e: e.activation(
            Tb[:], Tb[:], AF.Identity, bias=SCR[:, c + 5:c + 6], scale=SCR[:, c + 4:c + 5]), "SC")
        w("vector", "SC", sc_nrm)
        w("vector", "DV", x_war_dv)
        dv_xadd = op("vector", lambda e: e.tensor_tensor(Xt[:], Xt[:], Tb[:], ALU.add), "DV")
        return sc_nrm, dv_xadd

    pb.close()
    pc = ExitStack()
    PR = [pc.enter_context(nc.psum_tensor(f"pr{i}", [128, 1024], F32)) for i in range(4)]

    pe_retr = [0] * TN
    sc_cp = [0] * TN
    sc_sqa = [0] * TN
    dv_nrm_a = [0] * TN
    dv_xattn = [0] * TN

    def attn_stage1(t):
        p, pr = t % 4, t % 2
        return lnacc_s1(PR[p], T1A[pr], 6 * pr,
                        wait_pe=pe_retr[t],
                        t_free_dv=dv_xattn[t - 2] if t >= 2 else 0)

    def attn_stage2(t):
        pr = t % 2
        return lnacc_s2(T1A[pr], SQD[pr], 6 * pr, sc_cp[t])

    def attn_stage3(t):
        pr = t % 2
        nrm, xadd = lnacc_s3(T1A[pr], 6 * pr, X[t], sc_sqa[t], 0)
        dv_nrm_a[t] = nrm
        dv_xattn[t] = xadd

    for t in range(TN):
        p = t % 4
        w("tensor", "SC", dv_ztevac[NCH - 1][7])
        w("tensor", "DV", dv_hcg)
        if t >= 4:
            w("tensor", "SC", sc_cp[t - 4])
        for fc in range(2):
            for d in range(ND):
                op("tensor", lambda e, t=t, p=p, fc=fc, d=d:
                   e.matmul(PR[p][:, 512 * fc:512 * fc + 512],
                            ZT[d][:, 128 * t:128 * t + 128],
                            HC[d][:, 512 * fc:512 * fc + 512],
                            start=(d == 0), stop=False, skip_group_check=True))
            pe_retr[t] = op("tensor", lambda e, t=t, p=p, fc=fc:
                            e.matmul(PR[p][:, 512 * fc:512 * fc + 512], IDB[:],
                                     XW[t][:, 512 * fc:512 * fc + 512],
                                     start=False, stop=True, skip_group_check=True), "PE")
        if t >= 2:
            attn_stage3(t - 2)
        if t >= 1:
            sc_sqa[t - 1] = attn_stage2(t - 1)
        sc_cp[t] = attn_stage1(t)
    sc_sqa[TN - 1] = attn_stage2(TN - 1)
    attn_stage3(TN - 2)
    attn_stage3(TN - 1)
    if phases < 90:
        for i, t in enumerate(DUMP_TILES):
            w("vector", "DV", dv_xattn[t])
            op("vector", lambda e, i=i, c=6 * (t % 2): e.tensor_copy(STDMP[i][:, 0:6], SCR[:, c:c + 6]))

    if phases <= 3:
        return finish_debug(pc, at, dump=[(zt_o, ZT), (hc_o, HC), (xw_o, XW), (st_o, STDMP)])

    ffps_pr0 = PR[0]
    pc.close()
    at.close()

    ff = ExitStack()
    fsb = lambda n, shp, dt: ff.enter_context(nc.sbuf_tensor(n, shp, dt))
    X2T = fsb("x2t", [128, 8192], BF16)
    X2S = [fsb(f"x2s{i}", [128, 1024], BF16) for i in range(2)]
    SIL = [fsb(f"sil{i}", [128, 1024], BF16) for i in range(2)]
    T1FA = [fsb(f"t1f{i}", [128, 1024], F32) for i in range(2)]
    SQDF1 = fsb("sqdf", [128, 1024], BF16)
    SQDF = [SQDF1, SQDF1]
    WGU = [fsb(f"wgu{i}", [128, 2048], BF16) for i in range(2)]

    ffps = {"PR0": ffps_pr0}
    prev_dn_sc = 0
    slot_pe = [0, 0]
    prev_half_sc = sc_cp[15]
    prev_half_dv = dv_xattn[15]

    w("gpsimd", "PE", pe_retr[15])
    w("gpsimd", "DV", dv_xattn[15])
    w("gpsimd", "SC", sc_cp[15])
    w("scalar", "PE", pe_retr[15])
    w("scalar", "DV", dv_xattn[15])
    w("vector", "PE", pe_retr[15])
    w("vector", "SC", sc_cp[15])
    dv_xfin = [0] * TN

    def emit_ln2(hf, war_dv):
        tg0 = 8 * hf
        w("vector", "DV", war_dv)
        for tl in range(8):
            t = tg0 + tl
            p = 12 * (tl % 2)
            w("vector", "DV", dv_xattn[t])
            if tl >= 2:
                w("vector", "DV", agg_fin[tl - 2])
            op("vector", lambda e, t=t, p=p: e.bn_stats(STA[:, p:p + 6], X[t][:, 0:512]))
            stat_fin[tl] = op("vector", lambda e, t=t, p=p: e.bn_stats(STA[:, p + 6:p + 12], X[t][:, 512:1024]), "DV")
            if tl >= 1:
                q = 12 * ((tl - 1) % 2)
                w("vector", "DV", stat_fin[tl - 1])
                op("vector", lambda e, t=t - 1, q=q: e.bn_aggr(aggr_out(MV1, t), STA[:, q:q + 6]))
                gap("vector", 64)
                agg_fin[tl - 1] = op("vector", lambda e, t=t - 1, q=q: e.bn_aggr(aggr_out(MV2, t), STA[:, q + 6:q + 12]), "DV")
        w("vector", "DV", stat_fin[7])
        q = 12 * (7 % 2)
        op("vector", lambda e, t=tg0 + 7, q=q: e.bn_aggr(aggr_out(MV1, t), STA[:, q:q + 6]))
        gap("vector", 64)
        fin = op("vector", lambda e, t=tg0 + 7, q=q: e.bn_aggr(aggr_out(MV2, t), STA[:, q + 6:q + 12]), "DV")
        w("vector", "DV", fin)
        return ln_combine(tg0, 8, M1c, R2s, NB2s, fold=INV127)

    def emit_x2pipe(hf, dv_ln2, pe_before, prev_xt):
        tg0 = 8 * hf
        sc_x2 = [0] * 8
        xt = [0] * 8
        HN = "B" if hf else "A"
        for tl in range(8):
            t = tg0 + tl
            w("scalar", "DV", dv_ln2)
            if tl >= 2:
                w("scalar", f"X{HN}{tl % 2}", xt[tl - 2])
            elif prev_xt is not None:
                w("scalar", f"XA{tl % 2}", prev_xt[tl + 6])
            sc_x2[tl] = op("scalar", lambda e, t=t, tl=tl: e.activation(
                X2S[tl % 2][:], X[t][:], AF.Identity,
                bias=NB2s[:, t:t + 1], scale=R2s[:, t:t + 1]), "SC")
            w("sync", "SC", sc_x2[tl])
            if tl == 0:
                w("sync", "PE", pe_before)
            sem = f"X{HN}{tl % 2}"
            C[sem] += 16
            xt[tl] = C[sem]
            plan["sync"].append(lambda e, tl=tl, s=sem: e.dma_start_transpose(
                X2T[:, :].rearrange("p (d t) -> p d t", d=ND)[:, :, 128 * tl:128 * tl + 128],
                X2S[tl % 2][:, :].rearrange("p (d t) -> p d t", d=ND)).then_inc(S[s], 16))
        return sc_x2, xt

    dv_ln2_done = 0
    x2pipe_B = None

    for hf in range(2):
        tg0 = 8 * hf

        if hf == 0:
            dv_ln2_done = emit_ln2(0, 0)
            _warm_done = True
            for wv in (stat_fin[1], stat_fin[3], stat_fin[5], stat_fin[7],
                       agg_fin[6], dv_ln2_done):
                w("tensor", "DV", wv)
                op("tensor", lambda e: e.matmul(
                    ffps["PR0"][0:64, 0:64], XW[0][:, 0:64], XW[0][:, 0:64],
                    start=True, stop=True, skip_group_check=True))
            sc_x2, xt_cnt = emit_x2pipe(0, dv_ln2_done, pe_retr[15], None)
        else:
            sc_x2, xt_cnt = x2pipe_B
        if phases <= 4:
            return finish_debug(ff)

        if hf == 0:
            pg_scope = ExitStack()
            PG = [pg_scope.enter_context(nc.psum_tensor(f"pg{i}", [128, 1024], F32)) for i in range(2)]
            PU = [pg_scope.enter_context(nc.psum_tensor(f"pu{i}", [128, 1024], F32)) for i in range(2)]
            ffps["PG"], ffps["PU"] = PG, PU
        else:
            PG, PU = ffps["PG"], ffps["PU"]
        wt_gu = [0] * NH
        pe_g = [[0] * 2 for _ in range(NH)]
        pe_u = [[0] * 2 for _ in range(NH)]
        sc_sil = [[0] * 2 for _ in range(NH)]
        dv_ht = [[0] * 2 for _ in range(NH)]

        def HT(eta):
            return XW[eta] if eta < 16 else HC[eta - 16]

        for eta in range(NH):
            sl = eta % 2
            if hf == 1 and eta == 0:
                w("tensor", "SC", prev_dn_sc)
            w("gpsimd", "PE", pe_u[eta - 2][1] if eta >= 2 else slot_pe[sl])
            wt_gu[eta] = dma("gpsimd", WGU[sl][:], wgu[eta, :, :], f"WG{sl}")
            w("tensor", f"WG{sl}", wt_gu[eta])
            if eta < 2:
                hn = "B" if hf else "A"
                w("tensor", f"X{hn}0", xt_cnt[6])
                w("tensor", f"X{hn}1", xt_cnt[7])
            if eta >= 2:
                w("tensor", "SC", sc_sil[eta - 2][1])
                w("tensor", "DV", dv_ht[eta - 2][1])
            for c in range(2):
                for d in range(ND):
                    pe_g[eta][c] = op("tensor", lambda e, sl=sl, c=c, d=d: e.matmul(
                        PG[sl][:, 512 * c:512 * c + 512],
                        WGU[sl][:, 128 * d:128 * d + 128],
                        X2T[:, 1024 * d + 512 * c:1024 * d + 512 * c + 512],
                        start=(d == 0), stop=(d == ND - 1)),
                        "PE" if d == ND - 1 else None)
            for c in range(2):
                for d in range(ND):
                    pe_u[eta][c] = op("tensor", lambda e, sl=sl, c=c, d=d: e.matmul(
                        PU[sl][:, 512 * c:512 * c + 512],
                        WGU[sl][:, 1024 + 128 * d:1024 + 128 * d + 128],
                        X2T[:, 1024 * d + 512 * c:1024 * d + 512 * c + 512],
                        start=(d == 0), stop=(d == ND - 1)),
                        "PE" if d == ND - 1 else None)
            slot_pe[sl] = pe_u[eta][1]
            if hf == 0 and eta == 6:
                dv_ln2_done = emit_ln2(1, dv_ln2_done)
            for c in range(2):
                w("scalar", "PE", pe_g[eta][c])
                if eta >= 2:
                    w("scalar", "DV", dv_ht[eta - 2][c])
                sc_sil[eta][c] = op("scalar", lambda e, sl=sl, c=c: e.activation(
                    SIL[sl][:, 512 * c:512 * c + 512], PG[sl][:, 512 * c:512 * c + 512],
                    AF.Silu, bias=Z128[:], scale=1.0), "SC")
            for c in range(2):
                w("vector", "SC", sc_sil[eta][c])
                w("vector", "PE", pe_u[eta][c])
                dv_ht[eta][c] = op("vector", lambda e, eta=eta, sl=sl, c=c: e.scalar_tensor_tensor(
                    HT(eta)[:, 512 * c:512 * c + 512], SIL[sl][:, 512 * c:512 * c + 512],
                    INV127, PU[sl][:, 512 * c:512 * c + 512], ALU.mult, ALU.mult), "DV")
        if hf == 0:
            pg_scope.close()
        if phases <= 5:
            return finish_debug(ff)

        if hf == 0:
            pf_scope = ExitStack()
            PF = [pf_scope.enter_context(nc.psum_tensor(f"pf{i}", [128, 1024], F32)) for i in range(2)]
            ffps["PF"] = PF
        else:
            PF = ffps["PF"]
        PFS = [PF[0], PF[1], PU[0], PU[1]]
        pe_dn = [0] * 8
        sc_cpf = [0] * 8
        sc_sqf = [0] * 8
        dv_nrmf = [0] * 8

        def dn_stage1(tl):
            pr = tl % 2
            return lnacc_s1(PFS[tl % 4], T1FA[pr], 6 * pr,
                            wait_pe=pe_dn[tl],
                            t_free_dv=dv_xfin[tg0 + tl - 2] if tl >= 2 else
                            (dv_xfin[7] if hf == 1 else 0))

        def dn_stage2(tl):
            pr = tl % 2
            return lnacc_s2(T1FA[pr], SQDF[pr], 6 * pr, sc_cpf[tl])

        def dn_stage3(tl):
            t = tg0 + tl
            pr = tl % 2
            nrm, xadd = lnacc_s3(T1FA[pr], 6 * pr, X[t], sc_sqf[tl], dv_xattn[t])
            dv_nrmf[tl] = nrm
            dv_xfin[t] = xadd
            w("sync", "DV", dv_xfin[t])
            dma("sync", out[t, :, :], X[t][:], "ST")

        w("tensor", "WWD", wt_wd)
        w("tensor", "DV", dv_ht[NH - 1][1])
        w("tensor", "SC", sc_sil[NH - 1][1])
        for tl in range(8):
            t = tg0 + tl
            sl = tl % 4
            if tl >= 4:
                w("tensor", "SC", sc_cpf[tl - 4])
            for fc in range(2):
                for eta in range(NH):
                    pe_dn[tl] = op("tensor", lambda e, sl=sl, tl=tl, fc=fc, eta=eta: e.matmul(
                        PFS[sl][:, 512 * fc:512 * fc + 512],
                        HT(eta)[:, 128 * tl:128 * tl + 128],
                        WDTB[:, 1024 * eta + 512 * fc:1024 * eta + 512 * fc + 512],
                        start=(eta == 0), stop=(eta == NH - 1)),
                        "PE" if (eta == NH - 1 and fc == 1) else None)
            if tl >= 2:
                dn_stage3(tl - 2)
            if tl >= 1:
                sc_sqf[tl - 1] = dn_stage2(tl - 1)
            sc_cpf[tl] = dn_stage1(tl)
            if hf == 0 and tl == 5:
                x2pipe_B = emit_x2pipe(1, dv_ln2_done, pe_u[NH - 1][1], xt_cnt)
        sc_sqf[7] = dn_stage2(7)
        dn_stage3(6)
        dn_stage3(7)
        if hf == 0:
            pf_scope.close()
        if phases <= 6 and hf == 0:
            return finish_debug(ff)
        prev_dn_sc = sc_cpf[7]

    ff.close()

    w("sync", "ST", C["ST"])

    with nc.Block() as block:
        @block.sync
        def _(e):
            for f in plan["sync"]:
                f(e)

        @block.gpsimd
        def _(e):
            for f in plan["gpsimd"]:
                f(e)

        @block.vector
        def _(e):
            for f in plan["vector"]:
                f(e)

        @block.scalar
        def _(e):
            for f in plan["scalar"]:
                f(e)

        @block.tensor
        def _(e):
            for f in plan["tensor"]:
                f(e)

    es.close()
    return nc


def kernel(L0_attn_x, L0_attn_bind_key, L0_attn_unbind_key, L0_attn_gate, L0_attn_decay,
           L0_ffn_w_gate, L0_ffn_w_up, L0_ffn_w_down):
    x = np.ascontiguousarray(np.asarray(L0_attn_x), dtype=np.float32)
    b = np.asarray(L0_attn_bind_key)
    u = np.asarray(L0_attn_unbind_key)
    gate = np.asarray(L0_attn_gate).reshape(1)
    decay = np.asarray(L0_attn_decay).reshape(())
    wg = np.asarray(L0_ffn_w_gate)
    wu = np.asarray(L0_ffn_w_up)
    wd = np.asarray(L0_ffn_w_down)

    D = 1024
    idx = np.arange(D)
    circ = (idx[:, None] - idx[None, :]) % D
    cbt = np.ascontiguousarray(b[circ].reshape(ND, 128, D).transpose(1, 0, 2).reshape(128, ND * D))
    cuT = np.ascontiguousarray(u[circ].reshape(ND, 128, D).transpose(1, 0, 2).reshape(128, ND * D))

    ii = np.arange(128)[:, None]
    jj = np.arange(512)[None, :]
    e0m = np.maximum(jj - ii, 0).astype(np.float32)
    cstf = np.ascontiguousarray(np.concatenate(
        [e0m, np.broadcast_to(decay, (128, 1)).astype(np.float32),
         np.broadcast_to(gate, (128, 1)).astype(np.float32)], axis=1))
    cstb = np.ascontiguousarray(np.concatenate(
        [(jj >= ii).astype(ml_dtypes.bfloat16), np.eye(128, dtype=ml_dtypes.bfloat16)], axis=1))

    wgt = wg.reshape(NH, 128, ND, 128).transpose(0, 3, 2, 1).reshape(NH, 128, 1024)
    wut = wu.reshape(NH, 128, ND, 128).transpose(0, 3, 2, 1).reshape(NH, 128, 1024)
    wgu_h = np.ascontiguousarray(np.concatenate([wgt, wut], axis=2))
    wdt = np.ascontiguousarray(wd.T.reshape(NH, 128, 1024).transpose(1, 0, 2).reshape(128, NH * 1024))

    nc = build_nc()
    in_maps = []
    for c in range(8):
        in_maps.append(dict(
            xin=np.ascontiguousarray(x[c].reshape(TN, 128, 1024).transpose(1, 0, 2).reshape(128, TN * 1024)),
            cbt=cbt, cuT=cuT, cstf=cstf, cstb=cstb,
            wgu=wgu_h, wdt=wdt,
        ))
    res = run_bass_kernel_spmd(nc, in_maps, core_ids=list(range(8)))
    return np.stack([res.results[c]["out"].reshape(2048, 1024) for c in range(8)], axis=0)
